# version 01
# baseline (speedup 1.0000x reference)
import numpy as np
import concourse.bass as bass
import concourse.mybir as mybir
from concourse.bass_utils import run_bass_kernel_spmd

F32 = mybir.dt.float32
BF16 = mybir.dt.bfloat16
I32 = mybir.dt.int32
OP = mybir.AluOpType
AF = mybir.ActivationFunctionType
AX = mybir.AxisListType

NCORE = 8
NEG = -3.0e38

_prog_cache = {}


def _cached(key, builder):
    if key not in _prog_cache:
        _prog_cache[key] = builder()
    return _prog_cache[key]


def _run(nc, in_maps):
    return run_bass_kernel_spmd(nc, in_maps, core_ids=list(range(NCORE))).results


def _f32(x):
    return np.ascontiguousarray(x, dtype=np.float32)


class Sched:
    ENGINES = ("tensor", "vector", "scalar", "gpsimd", "sync")

    def __init__(self):
        self.steps = []

    def add(self, engine, fn, deps=(), dma=False):
        sid = len(self.steps)
        self.steps.append({"engine": engine, "fn": fn,
                           "deps": [d for d in deps if d is not None], "dma": dma})
        return sid

    def emit(self, nc):
        cnt = {e: 0 for e in self.ENGINES}
        pos = {}
        for i, s in enumerate(self.steps):
            cnt[s["engine"]] += 16 if s["dma"] else 1
            pos[i] = (s["engine"], cnt[s["engine"]])
        used = sorted({s["engine"] for s in self.steps},
                      key=self.ENGINES.index)
        with nc.Block() as block:
            sems = {}
            import contextlib
            stack = contextlib.ExitStack()
            for e in used:
                sems[e] = stack.enter_context(nc.semaphore(f"s_{e}"))
            waits_of = {e: set() for e in used}
            for s in self.steps:
                for d in s["deps"]:
                    de = self.steps[d]["engine"]
                    if de != s["engine"]:
                        waits_of[s["engine"]].add(de)
            for eng_name in used:
                sub = [(i, s) for i, s in enumerate(self.steps) if s["engine"] == eng_name]

                def body(eng, eng_name=eng_name, sub=sub):
                    for wsem in sorted(waits_of[eng_name]):
                        eng.sem_clear(sems[wsem])
                    for _ in range(10):
                        eng.nop(cycle_cnt=250)
                    waited = {}
                    for i, s in sub:
                        for d in s["deps"]:
                            de, dv = pos[d]
                            if de == eng_name or waited.get(de, 0) >= dv:
                                continue
                            eng.wait_ge(sems[de], dv)
                            waited[de] = dv
                        inst = s["fn"](eng)
                        amt = 16 if s["dma"] else 1
                        if inst is None:
                            eng.nop().then_inc(sems[eng_name], amt)
                        elif (not s["dma"]) and eng_name in ("vector", "scalar", "gpsimd"):
                            eng.drain().then_inc(sems[eng_name], amt)
                        else:
                            inst.then_inc(sems[eng_name], amt)

                getattr(block, eng_name)(body)
        return sems


def build_mm(K, M, N, bias, rhs_relu):
    KT, MT, NCH = K // 128, (M + 127) // 128, (N + 511) // 512
    nc = bass.Bass("TRN2", target_bir_lowering=False, debug=False)
    lhsT = nc.dram_tensor("lhsT", [K, M], F32, kind="ExternalInput").ap()
    rhs = nc.dram_tensor("rhs", [K, N], F32, kind="ExternalInput").ap()
    out = nc.dram_tensor("out", [M, N], F32, kind="ExternalOutput").ap()
    if bias:
        bias_in = nc.dram_tensor("bias", [MT, 128], F32, kind="ExternalInput").ap()
        ident_in = nc.dram_tensor("ident", [128, 128], F32, kind="ExternalInput").ap()
    with (
        nc.sbuf_tensor("lt", [128, KT * M], BF16) as lt,
        nc.sbuf_tensor("rt", [128, KT * N], BF16) as rt,
        nc.sbuf_tensor("ot0", [128, N], F32) as ot0,
        nc.sbuf_tensor("ot1", [128, N], F32) as ot1,
        nc.sbuf_tensor("bsb", [MT, 128] if bias else [1, 1], F32) as bsb,
        nc.sbuf_tensor("bcol", [128, MT] if bias else [1, 1], F32) as bcol,
        nc.sbuf_tensor("idn", [128, 128] if bias else [1, 1], F32) as idn,
        nc.psum_tensor("ps0", [128, 512], F32) as ps0,
        nc.psum_tensor("ps1", [128, 512], F32) as ps1,
        nc.psum_tensor("pbias", [128, MT] if bias else [1, 1], F32) as pbias,
    ):
        S = Sched()
        obufs, psums = [ot0, ot1], [ps0, ps1]
        loads = []
        for kt in range(KT):
            loads.append(S.add("gpsimd", lambda g, kt=kt: g.dma_start(
                out=lt[:, kt * M:(kt + 1) * M], in_=lhsT[kt * 128:(kt + 1) * 128, :]), dma=True))
        rloads = []
        for kt in range(KT):
            rloads.append(S.add("gpsimd", lambda g, kt=kt: g.dma_start(
                out=rt[:, kt * N:(kt + 1) * N], in_=rhs[kt * 128:(kt + 1) * 128, :]), dma=True))
        bias_ready = None
        if bias:
            lb = S.add("gpsimd", lambda g: g.dma_start(out=bsb[:, :], in_=bias_in[:, :]), dma=True)
            li = S.add("gpsimd", lambda g: g.dma_start(out=idn[:, :], in_=ident_in[:, :]), dma=True)
            tb = S.add("tensor", lambda t: t.transpose(pbias[:, 0:MT], bsb[:, :], idn[0:MT, 0:MT]), deps=[lb, li])
            bias_ready = S.add("scalar", lambda s: s.activation(bcol[:, :], pbias[:, :], AF.Copy), deps=[tb])
        relu_done = None
        if rhs_relu:
            def do_relu(v):
                last = None
                for kt in range(KT):
                    last = v.tensor_scalar(rt[:, kt * N:(kt + 1) * N],
                                           rt[:, kt * N:(kt + 1) * N], 0.0, None, OP.max)
                return last
            relu_done = S.add("vector", do_relu, deps=rloads)
        i = 0
        evacs, dmas = {}, {}
        for mt in range(MT):
            mrows = min(128, M - mt * 128)
            for nch in range(NCH):
                ncols = min(512, N - nch * 512)
                deps = list(loads) + ([relu_done] if rhs_relu else rloads)
                if i >= 2:
                    deps.append(evacs[i - 2])
                def do_mm(t, mt=mt, nch=nch, mrows=mrows, ncols=ncols, i=i):
                    pb = psums[i % 2]
                    last = None
                    for kt in range(KT):
                        last = t.matmul(pb[0:mrows, 0:ncols],
                                        lt[:, kt * M + mt * 128: kt * M + mt * 128 + mrows],
                                        rt[:, kt * N + nch * 512: kt * N + nch * 512 + ncols],
                                        start=(kt == 0), stop=(kt == KT - 1))
                    return last
                mm_s = S.add("tensor", do_mm, deps=deps)
                edeps = [mm_s] + ([bias_ready] if bias else [])
                if mt >= 2 and nch == 0:
                    edeps.append(dmas[mt - 2])
                def do_evac(v, mt=mt, nch=nch, mrows=mrows, ncols=ncols, i=i):
                    pb = psums[i % 2]
                    ob = obufs[mt % 2]
                    if bias:
                        return v.tensor_scalar(ob[0:mrows, nch * 512:nch * 512 + ncols],
                                               pb[0:mrows, 0:ncols],
                                               bcol[0:mrows, mt:mt + 1], None, OP.add)
                    return v.tensor_copy(ob[0:mrows, nch * 512:nch * 512 + ncols],
                                         pb[0:mrows, 0:ncols])
                evacs[i] = S.add("vector", do_evac, deps=edeps)
                i += 1
            dmas[mt] = S.add("sync", lambda sy, mt=mt, mrows=mrows: sy.dma_start(
                out=out[mt * 128:mt * 128 + mrows, :], in_=obufs[mt % 2][0:mrows, :]),
                deps=[evacs[mt * NCH + nch] for nch in range(NCH)], dma=True)
        S.add("sync", lambda sy: None, deps=[dmas[MT - 1]])
        S.emit(nc)
    return nc


def mm(lhsT_np, rhs_list, bias_np=None, rhs_relu=False):
    sh = lhsT_np[0].shape if isinstance(lhsT_np, list) else lhsT_np.shape
    K, M = sh
    N = rhs_list[0].shape[1]
    MT = (M + 127) // 128
    nc = _cached(("mm", K, M, N, bias_np is not None, rhs_relu),
                 lambda: build_mm(K, M, N, bias_np is not None, rhs_relu))
    if bias_np is not None:
        bpad = np.zeros((MT * 128,), np.float32)
        bpad[:M] = np.asarray(bias_np, np.float32).reshape(-1)
        bpad = _f32(bpad.reshape(MT, 128))
        eye = np.eye(128, dtype=np.float32)
    maps = []
    for c in range(NCORE):
        m = {"lhsT": _f32(lhsT_np[c] if isinstance(lhsT_np, list) else lhsT_np),
             "rhs": _f32(rhs_list[c])}
        if bias_np is not None:
            m["bias"], m["ident"] = bpad, eye
        maps.append(m)
    return [np.asarray(r["out"]) for r in _run(nc, maps)]


def build_ln(T, D, with_gb):
    TT = T // 128
    DCH = (D + 511) // 512
    nc = bass.Bass("TRN2", target_bir_lowering=False, debug=False)
    x = nc.dram_tensor("x", [T, D], F32, kind="ExternalInput").ap()
    out = nc.dram_tensor("out", [T, D], F32, kind="ExternalOutput").ap()
    if with_gb:
        g_in = nc.dram_tensor("g", [1, D], F32, kind="ExternalInput").ap()
        b_in = nc.dram_tensor("b", [1, D], F32, kind="ExternalInput").ap()
    with (
        nc.sbuf_tensor("xt0", [128, D], F32) as xt0,
        nc.sbuf_tensor("xt1", [128, D], F32) as xt1,
        nc.sbuf_tensor("sq", [128, D], F32) as sq,
        nc.sbuf_tensor("ot0", [128, D], F32) as ot0,
        nc.sbuf_tensor("ot1", [128, D], F32) as ot1,
        nc.sbuf_tensor("st", [128, 8], F32) as st,
        nc.sbuf_tensor("grow", [1, D] if with_gb else [1, 1], F32) as grow,
        nc.sbuf_tensor("brow", [1, D] if with_gb else [1, 1], F32) as brow,
        nc.sbuf_tensor("ones1", [1, 128], F32) as ones1,
        nc.sbuf_tensor("gbc", [128, D] if with_gb else [1, 1], F32) as gbc,
        nc.sbuf_tensor("bbc", [128, D] if with_gb else [1, 1], F32) as bbc,
        nc.psum_tensor("pgb", [128, 512] if with_gb else [1, 1], F32) as pgb,
    ):
        S = Sched()
        gb_ready = None
        if with_gb:
            lg = S.add("gpsimd", lambda g: g.dma_start(out=grow[:, :], in_=g_in[:, :]), dma=True)
            lb = S.add("gpsimd", lambda g: g.dma_start(out=brow[:, :], in_=b_in[:, :]), dma=True)
            on = S.add("vector", lambda v: v.memset(ones1[:, :], 1.0))
            prev = None
            for dch in range(DCH):
                nco = min(512, D - dch * 512)
                t1 = S.add("tensor", lambda t, dch=dch, nco=nco: t.matmul(
                    pgb[:, 0:nco], ones1[0:1, :], grow[0:1, dch * 512:dch * 512 + nco],
                    start=True, stop=True), deps=[lg, on] + ([prev] if prev else []))
                c1 = S.add("scalar", lambda s, dch=dch, nco=nco: s.activation(
                    gbc[:, dch * 512:dch * 512 + nco], pgb[:, 0:nco], AF.Copy), deps=[t1])
                t2 = S.add("tensor", lambda t, dch=dch, nco=nco: t.matmul(
                    pgb[:, 0:nco], ones1[0:1, :], brow[0:1, dch * 512:dch * 512 + nco],
                    start=True, stop=True), deps=[lb, c1])
                prev = S.add("scalar", lambda s, dch=dch, nco=nco: s.activation(
                    bbc[:, dch * 512:dch * 512 + nco], pgb[:, 0:nco], AF.Copy), deps=[t2])
            gb_ready = prev
        xts, obs = [xt0, xt1], [ot0, ot1]
        prev_apply, prev_dma = {}, {}
        for tt in range(TT):
            xb = xts[tt % 2]
            ob = obs[tt % 2]
            ldeps = [prev_apply[tt - 2]] if tt >= 2 else []
            ld = S.add("gpsimd", lambda g, tt=tt, xb=xb: g.dma_start(
                out=xb[:, :], in_=x[tt * 128:(tt + 1) * 128, :]), deps=ldeps, dma=True)
            sqs = S.add("scalar", lambda s, xb=xb: s.activation(sq[:, :], xb[:, :], AF.Square),
                        deps=[ld] + ([prev_apply[tt - 1]] if tt >= 1 else []))
            def stats(v, xb=xb):
                v.reduce_sum(st[:, 0:1], xb[:, :], AX.X)
                v.reduce_sum(st[:, 1:2], sq[:, :], AX.X)
                v.drain()
                v.tensor_scalar(st[:, 2:3], st[:, 0:1], 1.0 / D, None, OP.mult)
                v.tensor_scalar(st[:, 3:4], st[:, 1:2], 1.0 / D, None, OP.mult)
                v.drain()
                v.tensor_mul(st[:, 4:5], st[:, 2:3], st[:, 2:3])
                v.drain()
                v.tensor_sub(st[:, 4:5], st[:, 3:4], st[:, 4:5])
                v.drain()
                return v.tensor_scalar(st[:, 4:5], st[:, 4:5], 1e-5, None, OP.add)
            vs = S.add("vector", stats, deps=[ld, sqs])
            def rstd(s):
                s.activation(st[:, 6:7], st[:, 4:5], AF.Ln)
                s.drain()
                return s.activation(st[:, 5:6], st[:, 6:7], AF.Exp, scale=-0.5)
            rs = S.add("scalar", rstd, deps=[vs])
            adeps = [rs] + ([gb_ready] if with_gb else []) + ([prev_dma[tt - 2]] if tt >= 2 else [])
            def apply(v, xb=xb, ob=ob):
                last = v.tensor_scalar(ob[:, :], xb[:, :], st[:, 2:3], st[:, 5:6],
                                       OP.subtract, OP.mult)
                if with_gb:
                    v.drain()
                    v.tensor_mul(ob[:, :], ob[:, :], gbc[:, :])
                    v.drain()
                    last = v.tensor_add(ob[:, :], ob[:, :], bbc[:, :])
                return last
            prev_apply[tt] = S.add("vector", apply, deps=adeps)
            prev_dma[tt] = S.add("sync", lambda sy, tt=tt, ob=ob: sy.dma_start(
                out=out[tt * 128:(tt + 1) * 128, :], in_=ob[:, :]), deps=[prev_apply[tt]], dma=True)
        S.add("sync", lambda sy: None, deps=[prev_dma[TT - 1]])
        S.emit(nc)
    return nc


def ln(x_list, g=None, b=None):
    T, D = x_list[0].shape
    with_gb = g is not None
    nc = _cached(("ln", T, D, with_gb), lambda: build_ln(T, D, with_gb))
    maps = []
    for c in range(NCORE):
        m = {"x": _f32(x_list[c])}
        if with_gb:
            m["g"] = _f32(np.asarray(g).reshape(1, D))
            m["b"] = _f32(np.asarray(b).reshape(1, D))
        maps.append(m)
    return [np.asarray(r["out"]) for r in _run(nc, maps)]


def build_add(T, D, n_in, scaled):
    TT = T // 128
    nc = bass.Bass("TRN2", target_bir_lowering=False, debug=False)
    a = nc.dram_tensor("a", [T, D], F32, kind="ExternalInput").ap()
    b = nc.dram_tensor("b", [T, D], F32, kind="ExternalInput").ap()
    if n_in == 3:
        c = nc.dram_tensor("c", [T, D], F32, kind="ExternalInput").ap()
    if scaled:
        al = nc.dram_tensor("alpha", [1, 1], F32, kind="ExternalInput").ap()
    out = nc.dram_tensor("out", [T, D], F32, kind="ExternalOutput").ap()
    with (
        nc.sbuf_tensor("at", [128, D], F32) as at,
        nc.sbuf_tensor("bt", [128, D], F32) as bt,
        nc.sbuf_tensor("ct", [128, D] if n_in == 3 else [1, 1], F32) as ct,
        nc.sbuf_tensor("ot0", [128, D], F32) as ot0,
        nc.sbuf_tensor("ot1", [128, D], F32) as ot1,
        nc.sbuf_tensor("alr", [1, 1], F32) as alr,
        nc.sbuf_tensor("alc", [128, 1], F32) as alc,
        nc.sbuf_tensor("ones1", [1, 128], F32) as ones1,
        nc.psum_tensor("pal", [128, 1] if scaled else [1, 1], F32) as pal,
    ):
        S = Sched()
        al_ready = None
        if scaled:
            la = S.add("gpsimd", lambda g: g.dma_start(out=alr[:, :], in_=al[:, :]), dma=True)
            on = S.add("vector", lambda v: v.memset(ones1[:, :], 1.0))
            tm = S.add("tensor", lambda t: t.matmul(pal[:, :], ones1[0:1, :], alr[0:1, 0:1],
                                                    start=True, stop=True), deps=[la, on])
            al_ready = S.add("scalar", lambda s: s.activation(alc[:, :], pal[:, :], AF.Copy), deps=[tm])
        obs = [ot0, ot1]
        papl, pdma = {}, {}
        for tt in range(TT):
            ob = obs[tt % 2]
            ldeps = [papl[tt - 1]] if tt >= 1 else []
            l1 = S.add("gpsimd", lambda g, tt=tt: g.dma_start(
                out=at[:, :], in_=a[tt * 128:(tt + 1) * 128, :]), deps=ldeps, dma=True)
            l2 = S.add("gpsimd", lambda g, tt=tt: g.dma_start(
                out=bt[:, :], in_=b[tt * 128:(tt + 1) * 128, :]), dma=True)
            ls = [l1, l2]
            if n_in == 3:
                ls.append(S.add("gpsimd", lambda g, tt=tt: g.dma_start(
                    out=ct[:, :], in_=c[tt * 128:(tt + 1) * 128, :]), dma=True))
            adeps = ls + ([al_ready] if scaled else []) + ([pdma[tt - 2]] if tt >= 2 else [])
            def apply(v, ob=ob):
                if scaled:
                    last = v.scalar_tensor_tensor(ob[:, :], bt[:, :], alc[:, 0:1], at[:, :],
                                                  OP.mult, OP.add)
                else:
                    last = v.tensor_add(ob[:, :], at[:, :], bt[:, :])
                if n_in == 3:
                    last = v.tensor_add(ob[:, :], ob[:, :], ct[:, :])
                return last
            papl[tt] = S.add("vector", apply, deps=adeps)
            pdma[tt] = S.add("sync", lambda sy, tt=tt, ob=ob: sy.dma_start(
                out=out[tt * 128:(tt + 1) * 128, :], in_=ob[:, :]), deps=[papl[tt]], dma=True)
        S.add("sync", lambda sy: None, deps=[pdma[TT - 1]])
        S.emit(nc)
    return nc


def resadd(a_list, b_list, c_list=None, alpha_list=None):
    T, D = a_list[0].shape
    n_in = 3 if c_list is not None else 2
    scaled = alpha_list is not None
    nc = _cached(("add", T, D, n_in, scaled), lambda: build_add(T, D, n_in, scaled))
    maps = []
    for cc in range(NCORE):
        m = {"a": _f32(a_list[cc]), "b": _f32(b_list[cc])}
        if c_list is not None:
            m["c"] = _f32(c_list[cc])
        if scaled:
            m["alpha"] = _f32(np.asarray(alpha_list[cc]).reshape(1, 1))
        maps.append(m)
    return [np.asarray(r["out"]) for r in _run(nc, maps)]


def _bcast_cols(S, nc_ctx, row_ap, psum, col_sb, ones1, nvals, deps):
    on = S.add("vector", lambda v: v.memset(ones1[:, :], 1.0))
    tm = S.add("tensor", lambda t: t.matmul(psum[:, 0:nvals], ones1[0:1, :],
                                            row_ap, start=True, stop=True), deps=deps + [on])
    return S.add("scalar", lambda s: s.activation(col_sb[:, 0:nvals], psum[:, 0:nvals], AF.Copy),
                 deps=[tm])


def build_premax(R, C, n_in, op_name):
    RT = (R + 127) // 128
    red_op = OP.max if op_name == "max" else OP.add
    nc = bass.Bass("TRN2", target_bir_lowering=False, debug=False)
    ins = [nc.dram_tensor(f"x{j}", [R, C], F32, kind="ExternalInput").ap() for j in range(n_in)]
    ident_in = nc.dram_tensor("ident", [128, 128], F32, kind="ExternalInput").ap()
    outs = [nc.dram_tensor(f"p{j}", [1, 1], F32, kind="ExternalOutput").ap() for j in range(n_in)]
    with (
        nc.sbuf_tensor("xt0", [128, C], F32) as xt0,
        nc.sbuf_tensor("xt1", [128, C], F32) as xt1,
        nc.sbuf_tensor("mcols", [128, RT], F32) as mcols,
        nc.sbuf_tensor("idn", [128, 128], F32) as idn,
        nc.sbuf_tensor("acc", [128, 1], F32) as acc,
        nc.sbuf_tensor("accr", [1, 128], F32) as accr,
        nc.sbuf_tensor("res", [1, 1], F32) as res,
        nc.psum_tensor("pt", [1, 128], F32) as pt,
    ):
        S = Sched()
        li = S.add("gpsimd", lambda g: g.dma_start(out=idn[:, :], in_=ident_in[:, :]), dma=True)
        xts = [xt0, xt1]
        prev_out = None
        for j in range(n_in):
            reds = {}
            for rt_i in range(RT):
                rr = min(128, R - rt_i * 128)
                xb = xts[rt_i % 2]
                ld = S.add("gpsimd", lambda g, j=j, rt_i=rt_i, rr=rr, xb=xb: g.dma_start(
                    out=xb[0:rr, :], in_=ins[j][rt_i * 128:rt_i * 128 + rr, :]),
                    deps=[reds[rt_i - 2]] if rt_i >= 2 else ([prev_out] if prev_out is not None else []),
                    dma=True)
                def dored(v, rt_i=rt_i, rr=rr, xb=xb):
                    if rr < 128:
                        v.memset(mcols[:, rt_i:rt_i + 1], 0.0)
                        v.drain()
                    return v.tensor_reduce(mcols[0:rr, rt_i:rt_i + 1], xb[0:rr, :], AX.X,
                                           red_op, apply_absolute_value=True)
                reds[rt_i] = S.add("vector", dored, deps=[ld])
            fin = S.add("vector", lambda v: v.tensor_reduce(
                acc[:, 0:1], mcols[:, 0:RT], AX.X, red_op),
                deps=list(reds.values()) + ([prev_tr] if j > 0 else []))
            tr = S.add("tensor", lambda t: t.transpose(pt[:, :], acc[:, 0:1], idn[:, :]),
                       deps=[fin, li])
            cp = S.add("scalar", lambda s: s.activation(accr[:, :], pt[0:1, :], AF.Copy),
                       deps=[tr] + ([prev_rd] if j > 0 else []))
            rd = S.add("vector", lambda v: v.tensor_reduce(res[0:1, 0:1], accr[0:1, :], AX.X, red_op),
                       deps=[cp] + ([prev_out] if j > 0 else []))
            prev_tr, prev_rd = tr, rd
            prev_out = S.add("sync", lambda sy, j=j: sy.dma_start(out=outs[j][:, :], in_=res[:, :]),
                             deps=[rd], dma=True)
        S.add("sync", lambda sy: None, deps=[prev_out])
        S.emit(nc)
    return nc


def premax(tensors_lists, op_name="max"):
    n_in = len(tensors_lists)
    R, C = tensors_lists[0][0].shape
    nc = _cached(("premax", R, C, n_in, op_name), lambda: build_premax(R, C, n_in, op_name))
    eye = np.eye(128, dtype=np.float32)
    maps = []
    for c in range(NCORE):
        m = {"ident": eye}
        for j in range(n_in):
            m[f"x{j}"] = _f32(tensors_lists[j][c])
        maps.append(m)
    res = _run(nc, maps)
    return [np.concatenate([np.asarray(res[c][f"p{j}"]).reshape(1) for c in range(NCORE)]).reshape(1, 8)
            for j in range(n_in)]


def build_quant_global(R, C, qmax):
    RT = (R + 127) // 128
    nc = bass.Bass("TRN2", target_bir_lowering=False, debug=False)
    x = nc.dram_tensor("x", [R, C], F32, kind="ExternalInput").ap()
    parts = nc.dram_tensor("parts", [1, 8], F32, kind="ExternalInput").ap()
    sprev = nc.dram_tensor("sprev", [1, 1], F32, kind="ExternalInput").ap()
    out = nc.dram_tensor("out", [R, C], F32, kind="ExternalOutput").ap()
    alpha = nc.dram_tensor("alpha", [1, 1], F32, kind="ExternalOutput").ap()
    with (
        nc.sbuf_tensor("xt0", [128, C], F32) as xt0,
        nc.sbuf_tensor("xt1", [128, C], F32) as xt1,
        nc.sbuf_tensor("it", [128, C], I32) as it,
        nc.sbuf_tensor("ot0", [128, C], F32) as ot0,
        nc.sbuf_tensor("ot1", [128, C], F32) as ot1,
        nc.sbuf_tensor("pr", [1, 8], F32) as pr,
        nc.sbuf_tensor("spv", [1, 1], F32) as spv,
        nc.sbuf_tensor("sc", [1, 4], F32) as sc,
        nc.sbuf_tensor("ones1", [1, 128], F32) as ones1,
        nc.sbuf_tensor("mcol", [128, 1], F32) as mcol,
        nc.psum_tensor("pm", [128, 1], F32) as pm,
    ):
        S = Sched()
        l1 = S.add("gpsimd", lambda g: g.dma_start(out=pr[:, :], in_=parts[:, :]), dma=True)
        l2 = S.add("gpsimd", lambda g: g.dma_start(out=spv[:, :], in_=sprev[:, :]), dma=True)

        def scales(v):
            v.tensor_reduce(sc[0:1, 0:1], pr[0:1, :], AX.X, OP.max)
            v.drain()
            v.tensor_mul(sc[0:1, 1:2], sc[0:1, 0:1], spv[0:1, 0:1])
            v.drain()
            v.tensor_scalar(sc[0:1, 1:2], sc[0:1, 1:2], 1.0 / qmax, 1e-12, OP.mult, OP.max)
            v.drain()
            v.reciprocal(sc[0:1, 2:3], sc[0:1, 1:2])
            v.drain()
            return v.tensor_mul(sc[0:1, 2:3], sc[0:1, 2:3], spv[0:1, 0:1])
        sv = S.add("vector", scales, deps=[l1, l2])
        bc = _bcast_cols(S, nc, sc[0:1, 2:3], pm, mcol, ones1, 1, [sv])
        adma = S.add("sync", lambda sy: sy.dma_start(out=alpha[:, :], in_=sc[0:1, 1:2]),
                     deps=[sv], dma=True)
        xts, ots = [xt0, xt1], [ot0, ot1]
        qs, dmas = {}, {}
        for rt_i in range(RT):
            rr = min(128, R - rt_i * 128)
            xb, ob = xts[rt_i % 2], ots[rt_i % 2]
            ld = S.add("gpsimd", lambda g, rt_i=rt_i, rr=rr, xb=xb: g.dma_start(
                out=xb[0:rr, :], in_=x[rt_i * 128:rt_i * 128 + rr, :]),
                deps=[qs[rt_i - 2]] if rt_i >= 2 else [], dma=True)
            def doq(v, rr=rr, xb=xb, ob=ob):
                v.tensor_scalar(it[0:rr, :], xb[0:rr, :], mcol[0:rr, 0:1], None, OP.mult)
                return v.tensor_copy(ob[0:rr, :], it[0:rr, :])
            qs[rt_i] = S.add("vector", doq,
                             deps=[ld, bc] + ([dmas[rt_i - 2]] if rt_i >= 2 else []))
            dmas[rt_i] = S.add("sync", lambda sy, rt_i=rt_i, rr=rr, ob=ob: sy.dma_start(
                out=out[rt_i * 128:rt_i * 128 + rr, :], in_=ob[0:rr, :]), deps=[qs[rt_i]], dma=True)
        S.add("sync", lambda sy: None, deps=[dmas[RT - 1], adma])
        S.emit(nc)
    return nc


def quant_global(x_list, parts, sprev_list, qmax):
    R, C = x_list[0].shape
    nc = _cached(("qg", R, C, qmax), lambda: build_quant_global(R, C, qmax))
    maps = [{"x": _f32(x_list[c]), "parts": _f32(parts),
             "sprev": _f32(np.asarray(sprev_list[c]).reshape(1, 1))} for c in range(NCORE)]
    res = _run(nc, maps)
    return ([np.asarray(r["out"]) for r in res], [np.asarray(r["alpha"]) for r in res])


def build_ffn0_mid1(R, C):
    RT = R // 128
    nc = bass.Bass("TRN2", target_bir_lowering=False, debug=False)
    upr = nc.dram_tensor("upr", [R, C], F32, kind="ExternalInput").ap()
    gtr = nc.dram_tensor("gtr", [R, C], F32, kind="ExternalInput").ap()
    pu = nc.dram_tensor("pu", [1, 8], F32, kind="ExternalInput").ap()
    pg = nc.dram_tensor("pg", [1, 8], F32, kind="ExternalInput").ap()
    s1 = nc.dram_tensor("s1", [1, 1], F32, kind="ExternalInput").ap()
    upq = nc.dram_tensor("upq", [R, C], F32, kind="ExternalOutput").ap()
    pint = nc.dram_tensor("pint", [R, C], F32, kind="ExternalOutput").ap()
    pmax = nc.dram_tensor("pmax", [1, 1], F32, kind="ExternalOutput").ap()
    s2o = nc.dram_tensor("s2o", [1, 1], F32, kind="ExternalOutput").ap()
    s3o = nc.dram_tensor("s3o", [1, 1], F32, kind="ExternalOutput").ap()
    ident_in = nc.dram_tensor("ident", [128, 128], F32, kind="ExternalInput").ap()
    import contextlib
    with contextlib.ExitStack() as _st:
        ut0 = _st.enter_context(nc.sbuf_tensor("ut0", [128, C], F32))
        ut1 = _st.enter_context(nc.sbuf_tensor("ut1", [128, C], F32))
        gt0 = _st.enter_context(nc.sbuf_tensor("gt0", [128, C], F32))
        gt1 = _st.enter_context(nc.sbuf_tensor("gt1", [128, C], F32))
        it = _st.enter_context(nc.sbuf_tensor("it", [128, C], I32))
        uq0 = _st.enter_context(nc.sbuf_tensor("uq0", [128, C], F32))
        uq1 = _st.enter_context(nc.sbuf_tensor("uq1", [128, C], F32))
        gq = _st.enter_context(nc.sbuf_tensor("gq", [128, C], F32))
        sgm = _st.enter_context(nc.sbuf_tensor("sgm", [128, C], F32))
        srf = _st.enter_context(nc.sbuf_tensor("srf", [128, C], F32))
        pp0 = _st.enter_context(nc.sbuf_tensor("pp0", [128, C], F32))
        pp1 = _st.enter_context(nc.sbuf_tensor("pp1", [128, C], F32))
        rows = _st.enter_context(nc.sbuf_tensor("rows", [1, 8], F32))
        sc = _st.enter_context(nc.sbuf_tensor("sc", [1, 8], F32))
        cols = _st.enter_context(nc.sbuf_tensor("cols", [128, 3], F32))
        mcols = _st.enter_context(nc.sbuf_tensor("mcols", [128, RT], F32))
        ones1 = _st.enter_context(nc.sbuf_tensor("ones1", [1, 128], F32))
        acc = _st.enter_context(nc.sbuf_tensor("acc", [128, 1], F32))
        accr = _st.enter_context(nc.sbuf_tensor("accr", [1, 128], F32))
        res = _st.enter_context(nc.sbuf_tensor("res", [1, 1], F32))
        idn = _st.enter_context(nc.sbuf_tensor("idn", [128, 128], F32))
        pm = _st.enter_context(nc.psum_tensor("pm", [128, 3], F32))
        pt = _st.enter_context(nc.psum_tensor("pt", [1, 128], F32))
        S = Sched()
        l1 = S.add("gpsimd", lambda g: g.dma_start(out=rows[0:1, 0:8], in_=pu[:, :]), dma=True)
        l2 = S.add("gpsimd", lambda g: g.dma_start(out=sc[0:1, 6:7], in_=s1[:, :]), dma=True)
        l3 = S.add("gpsimd", lambda g: g.dma_start(out=idn[:, :], in_=ident_in[:, :]), dma=True)
        def scales(v):
            v.tensor_reduce(sc[0:1, 0:1], rows[0:1, 0:8], AX.X, OP.max)
            v.drain()
            v.tensor_mul(sc[0:1, 1:2], sc[0:1, 0:1], sc[0:1, 6:7])
            v.drain()
            v.tensor_scalar(sc[0:1, 1:2], sc[0:1, 1:2], 1.0 / 32767.0, 1e-12, OP.mult, OP.max)
            v.drain()
            v.reciprocal(sc[0:1, 2:3], sc[0:1, 1:2])
            v.drain()
            return v.tensor_mul(sc[0:1, 2:3], sc[0:1, 2:3], sc[0:1, 6:7])
        sv2 = S.add("vector", scales, deps=[l1, l2])
        l5 = S.add("gpsimd", lambda g: g.dma_start(out=rows[0:1, 0:8], in_=pg[:, :]),
                   deps=[sv2], dma=True)

        def scales3(v):
            v.tensor_reduce(sc[0:1, 3:4], rows[0:1, 0:8], AX.X, OP.max)
            v.drain()
            v.tensor_mul(sc[0:1, 4:5], sc[0:1, 3:4], sc[0:1, 6:7])
            v.drain()
            v.tensor_scalar(sc[0:1, 4:5], sc[0:1, 4:5], 1.0 / 32767.0, 1e-12, OP.mult, OP.max)
            v.drain()
            v.reciprocal(sc[0:1, 5:6], sc[0:1, 4:5])
            v.drain()
            return v.tensor_mul(sc[0:1, 5:6], sc[0:1, 5:6], sc[0:1, 6:7])
        sv3 = S.add("vector", scales3, deps=[l5])
        on = S.add("vector", lambda v: v.memset(ones1[:, :], 1.0))
        tm1 = S.add("tensor", lambda t: t.matmul(pm[:, 0:1], ones1[0:1, :], sc[0:1, 2:3],
                                                 start=True, stop=True), deps=[sv2, on])
        tm2 = S.add("tensor", lambda t: t.matmul(pm[:, 1:2], ones1[0:1, :], sc[0:1, 5:6],
                                                 start=True, stop=True), deps=[sv3, on])
        tm3 = S.add("tensor", lambda t: t.matmul(pm[:, 2:3], ones1[0:1, :], sc[0:1, 4:5],
                                                 start=True, stop=True), deps=[sv3, on])
        bc = S.add("scalar", lambda s: s.activation(cols[:, 0:3], pm[:, 0:3], AF.Copy),
                   deps=[tm1, tm2, tm3])
        d_s2 = S.add("sync", lambda sy: sy.dma_start(out=s2o[:, :], in_=sc[0:1, 1:2]),
                     deps=[sv2], dma=True)
        d_s3 = S.add("sync", lambda sy: sy.dma_start(out=s3o[:, :], in_=sc[0:1, 4:5]),
                     deps=[sv3], dma=True)

        uts, gts, uqs, pps = [ut0, ut1], [gt0, gt1], [uq0, uq1], [pp0, pp1]
        vsteps, d1s, d2s = {}, {}, {}
        for rt_i in range(RT):
            ub, gb, uqb, ppb = uts[rt_i % 2], gts[rt_i % 2], uqs[rt_i % 2], pps[rt_i % 2]
            dd = [vsteps[rt_i - 2]] if rt_i >= 2 else []
            lu = S.add("gpsimd", lambda g, rt_i=rt_i, ub=ub: g.dma_start(
                out=ub[:, :], in_=upr[rt_i * 128:(rt_i + 1) * 128, :]), deps=dd, dma=True)
            lg = S.add("gpsimd", lambda g, rt_i=rt_i, gb=gb: g.dma_start(
                out=gb[:, :], in_=gtr[rt_i * 128:(rt_i + 1) * 128, :]), deps=dd, dma=True)
            def q1(v, ub=ub, uqb=uqb, gb=gb):
                v.tensor_scalar(it[:, :], ub[:, :], cols[:, 0:1], None, OP.mult)
                v.tensor_copy(uqb[:, :], it[:, :])
                v.drain()
                v.tensor_scalar(it[:, :], gb[:, :], cols[:, 1:2], None, OP.mult)
                return v.tensor_copy(gq[:, :], it[:, :])
            q1deps = [lu, lg, bc] + ([d1s[rt_i - 2], d2s[rt_i - 2]] if rt_i >= 2 else [])
            v1 = S.add("vector", q1, deps=q1deps)
            sg = S.add("scalar", lambda s: s.activation(sgm[:, :], gq[:, :], AF.Sigmoid,
                                                        scale=cols[:, 2:3]), deps=[v1])
            def q2(v, uqb=uqb, ppb=ppb, rt_i=rt_i):
                v.tensor_scalar(it[:, :], sgm[:, :], 65536.0, None, OP.mult)
                v.tensor_copy(srf[:, :], it[:, :])
                v.drain()
                v.tensor_scalar(srf[:, :], srf[:, :], 65535.0, None, OP.min)
                v.tensor_mul(ppb[:, :], gq[:, :], srf[:, :])
                v.drain()
                return v.tensor_reduce(mcols[:, rt_i:rt_i + 1], ppb[:, :], AX.X, OP.max,
                                       apply_absolute_value=True)
            vsteps[rt_i] = S.add("vector", q2, deps=[sg])
            d1s[rt_i] = S.add("sync", lambda sy, rt_i=rt_i, uqb=uqb: sy.dma_start(
                out=upq[rt_i * 128:(rt_i + 1) * 128, :], in_=uqb[:, :]), deps=[v1], dma=True)
            d2s[rt_i] = S.add("sync", lambda sy, rt_i=rt_i, ppb=ppb: sy.dma_start(
                out=pint[rt_i * 128:(rt_i + 1) * 128, :], in_=ppb[:, :]), deps=[vsteps[rt_i]], dma=True)
        fin = S.add("vector", lambda v: v.tensor_reduce(acc[:, 0:1], mcols[:, 0:RT], AX.X, OP.max),
                    deps=list(vsteps.values()))
        tr = S.add("tensor", lambda t: t.transpose(pt[:, :], acc[:, 0:1], idn[:, :]), deps=[fin, l3])
        cp = S.add("scalar", lambda s: s.activation(accr[:, :], pt[0:1, :], AF.Copy), deps=[tr])
        rd = S.add("vector", lambda v: v.tensor_reduce(res[0:1, 0:1], accr[0:1, :], AX.X, OP.max),
                   deps=[cp])
        dp = S.add("sync", lambda sy: sy.dma_start(out=pmax[:, :], in_=res[:, :]), deps=[rd], dma=True)
        S.add("sync", lambda sy: None, deps=[dp, d_s2, d_s3, d1s[RT - 1], d2s[RT - 1]])
        S.emit(nc)
    return nc


def ffn0_mid1(upr_list, gtr_list, pu, pg, s1_list):
    R, C = upr_list[0].shape
    nc = _cached(("f0m1", R, C), lambda: build_ffn0_mid1(R, C))
    eye = np.eye(128, dtype=np.float32)
    maps = [{"upr": _f32(upr_list[c]), "gtr": _f32(gtr_list[c]), "pu": _f32(pu), "pg": _f32(pg),
             "s1": _f32(np.asarray(s1_list[c]).reshape(1, 1)), "ident": eye} for c in range(NCORE)]
    res = _run(nc, maps)
    return ([np.asarray(r["upq"]) for r in res], [np.asarray(r["pint"]) for r in res],
            np.concatenate([np.asarray(res[c]["pmax"]).reshape(1) for c in range(NCORE)]).reshape(1, 8),
            [np.asarray(r["s2o"]) for r in res], [np.asarray(r["s3o"]) for r in res])


def build_ffn0_mid2(R, C):
    RT = R // 128
    nc = bass.Bass("TRN2", target_bir_lowering=False, debug=False)
    pintd = nc.dram_tensor("pint", [R, C], F32, kind="ExternalInput").ap()
    upqd = nc.dram_tensor("upq", [R, C], F32, kind="ExternalInput").ap()
    pp = nc.dram_tensor("pp", [1, 8], F32, kind="ExternalInput").ap()
    s2 = nc.dram_tensor("s2", [1, 1], F32, kind="ExternalInput").ap()
    s3 = nc.dram_tensor("s3", [1, 1], F32, kind="ExternalInput").ap()
    ident_in = nc.dram_tensor("ident", [128, 128], F32, kind="ExternalInput").ap()
    opre = nc.dram_tensor("opre", [R, C], F32, kind="ExternalOutput").ap()
    omax = nc.dram_tensor("omax", [1, 1], F32, kind="ExternalOutput").ap()
    s4o = nc.dram_tensor("s4o", [1, 1], F32, kind="ExternalOutput").ap()
    import contextlib
    with contextlib.ExitStack() as _st:
        pt0 = _st.enter_context(nc.sbuf_tensor("pt0", [128, C], F32))
        pt1 = _st.enter_context(nc.sbuf_tensor("pt1", [128, C], F32))
        ut0 = _st.enter_context(nc.sbuf_tensor("ut0", [128, C], F32))
        ut1 = _st.enter_context(nc.sbuf_tensor("ut1", [128, C], F32))
        it = _st.enter_context(nc.sbuf_tensor("it", [128, C], I32))
        g2 = _st.enter_context(nc.sbuf_tensor("g2", [128, C], F32))
        ob0 = _st.enter_context(nc.sbuf_tensor("ob0", [128, C], F32))
        ob1 = _st.enter_context(nc.sbuf_tensor("ob1", [128, C], F32))
        rows = _st.enter_context(nc.sbuf_tensor("rows", [1, 8], F32))
        sc = _st.enter_context(nc.sbuf_tensor("sc", [1, 8], F32))
        cols = _st.enter_context(nc.sbuf_tensor("cols", [128, 1], F32))
        mcols = _st.enter_context(nc.sbuf_tensor("mcols", [128, RT], F32))
        ones1 = _st.enter_context(nc.sbuf_tensor("ones1", [1, 128], F32))
        acc = _st.enter_context(nc.sbuf_tensor("acc", [128, 1], F32))
        accr = _st.enter_context(nc.sbuf_tensor("accr", [1, 128], F32))
        res = _st.enter_context(nc.sbuf_tensor("res", [1, 1], F32))
        idn = _st.enter_context(nc.sbuf_tensor("idn", [128, 128], F32))
        pm = _st.enter_context(nc.psum_tensor("pm", [128, 1], F32))
        ptp = _st.enter_context(nc.psum_tensor("ptp", [1, 128], F32))
        S = Sched()
        l1 = S.add("gpsimd", lambda g: g.dma_start(out=rows[0:1, 0:8], in_=pp[:, :]), dma=True)
        l2 = S.add("gpsimd", lambda g: g.dma_start(out=sc[0:1, 6:7], in_=s2[:, :]), dma=True)
        l3 = S.add("gpsimd", lambda g: g.dma_start(out=sc[0:1, 5:6], in_=s3[:, :]), dma=True)
        l4 = S.add("gpsimd", lambda g: g.dma_start(out=idn[:, :], in_=ident_in[:, :]), dma=True)

        def scales(v):
            v.tensor_reduce(sc[0:1, 0:1], rows[0:1, 0:8], AX.X, OP.max)
            v.tensor_scalar(sc[0:1, 7:8], sc[0:1, 5:6], 1.0 / 65536.0, None, OP.mult)
            v.drain()
            v.tensor_mul(sc[0:1, 1:2], sc[0:1, 0:1], sc[0:1, 7:8])
            v.drain()
            v.tensor_scalar(sc[0:1, 1:2], sc[0:1, 1:2], 1.0 / 32767.0, 1e-12, OP.mult, OP.max)
            v.drain()
            v.reciprocal(sc[0:1, 2:3], sc[0:1, 1:2])
            v.drain()
            return v.tensor_mul(sc[0:1, 2:3], sc[0:1, 2:3], sc[0:1, 7:8])
        sv = S.add("vector", scales, deps=[l1, l2, l3])
        on = S.add("vector", lambda v: v.memset(ones1[:, :], 1.0))
        tm = S.add("tensor", lambda t: t.matmul(pm[:, 0:1], ones1[0:1, :], sc[0:1, 2:3],
                                                start=True, stop=True), deps=[sv, on])
        bc = S.add("scalar", lambda s: s.activation(cols[:, 0:1], pm[:, 0:1], AF.Copy), deps=[tm])
        d_s4 = S.add("sync", lambda sy: sy.dma_start(out=s4o[:, :], in_=sc[0:1, 1:2]),
                     deps=[sv], dma=True)
        pts, uts, obs = [pt0, pt1], [ut0, ut1], [ob0, ob1]
        vsteps, odmas = {}, {}
        for rt_i in range(RT):
            pb, ub, ob = pts[rt_i % 2], uts[rt_i % 2], obs[rt_i % 2]
            dd = [vsteps[rt_i - 2]] if rt_i >= 2 else []
            lp = S.add("gpsimd", lambda g, rt_i=rt_i, pb=pb: g.dma_start(
                out=pb[:, :], in_=pintd[rt_i * 128:(rt_i + 1) * 128, :]), deps=dd, dma=True)
            lu = S.add("gpsimd", lambda g, rt_i=rt_i, ub=ub: g.dma_start(
                out=ub[:, :], in_=upqd[rt_i * 128:(rt_i + 1) * 128, :]), deps=dd, dma=True)
            def doq(v, pb=pb, ub=ub, ob=ob, rt_i=rt_i):
                v.tensor_scalar(it[:, :], pb[:, :], cols[:, 0:1], None, OP.mult)
                v.tensor_copy(g2[:, :], it[:, :])
                v.drain()
                v.tensor_mul(ob[:, :], g2[:, :], ub[:, :])
                v.drain()
                return v.tensor_reduce(mcols[:, rt_i:rt_i + 1], ob[:, :], AX.X, OP.max,
                                       apply_absolute_value=True)
            vsteps[rt_i] = S.add("vector", doq,
                                 deps=[lp, lu, bc] + ([odmas[rt_i - 2]] if rt_i >= 2 else []))
            odmas[rt_i] = S.add("sync", lambda sy, rt_i=rt_i, ob=ob: sy.dma_start(
                out=opre[rt_i * 128:(rt_i + 1) * 128, :], in_=ob[:, :]), deps=[vsteps[rt_i]], dma=True)
        fin = S.add("vector", lambda v: v.tensor_reduce(acc[:, 0:1], mcols[:, 0:RT], AX.X, OP.max),
                    deps=list(vsteps.values()))
        tr = S.add("tensor", lambda t: t.transpose(ptp[:, :], acc[:, 0:1], idn[:, :]), deps=[fin, l4])
        cp = S.add("scalar", lambda s: s.activation(accr[:, :], ptp[0:1, :], AF.Copy), deps=[tr])
        rd = S.add("vector", lambda v: v.tensor_reduce(res[0:1, 0:1], accr[0:1, :], AX.X, OP.max),
                   deps=[cp])
        dp = S.add("sync", lambda sy: sy.dma_start(out=omax[:, :], in_=res[:, :]), deps=[rd], dma=True)
        S.add("sync", lambda sy: None, deps=[dp, d_s4, odmas[RT - 1]])
        S.emit(nc)
    return nc


def ffn0_mid2(pint_list, upq_list, pp, s2_list, s3_list):
    R, C = pint_list[0].shape
    nc = _cached(("f0m2", R, C), lambda: build_ffn0_mid2(R, C))
    eye = np.eye(128, dtype=np.float32)
    maps = [{"pint": _f32(pint_list[c]), "upq": _f32(upq_list[c]), "pp": _f32(pp),
             "s2": _f32(np.asarray(s2_list[c]).reshape(1, 1)),
             "s3": _f32(np.asarray(s3_list[c]).reshape(1, 1)), "ident": eye} for c in range(NCORE)]
    res = _run(nc, maps)
    return ([np.asarray(r["opre"]) for r in res],
            np.concatenate([np.asarray(res[c]["omax"]).reshape(1) for c in range(NCORE)]).reshape(1, 8),
            [np.asarray(r["s4o"]) for r in res])


def build_ffn0_mid3(R, C):
    RT = R // 128
    nc = bass.Bass("TRN2", target_bir_lowering=False, debug=False)
    opred = nc.dram_tensor("opre", [R, C], F32, kind="ExternalInput").ap()
    po = nc.dram_tensor("po", [1, 8], F32, kind="ExternalInput").ap()
    s4 = nc.dram_tensor("s4", [1, 1], F32, kind="ExternalInput").ap()
    s2 = nc.dram_tensor("s2", [1, 1], F32, kind="ExternalInput").ap()
    oint = nc.dram_tensor("oint", [R, C], F32, kind="ExternalOutput").ap()
    alpha = nc.dram_tensor("alpha", [1, 1], F32, kind="ExternalOutput").ap()
    with (
        nc.sbuf_tensor("pt0", [128, C], F32) as pt0,
        nc.sbuf_tensor("pt1", [128, C], F32) as pt1,
        nc.sbuf_tensor("it", [128, C], I32) as it,
        nc.sbuf_tensor("ob0", [128, C], F32) as ob0,
        nc.sbuf_tensor("ob1", [128, C], F32) as ob1,
        nc.sbuf_tensor("rows", [1, 8], F32) as rows,
        nc.sbuf_tensor("sc", [1, 8], F32) as sc,
        nc.sbuf_tensor("cols", [128, 1], F32) as cols,
        nc.sbuf_tensor("ones1", [1, 128], F32) as ones1,
        nc.psum_tensor("pm", [128, 1], F32) as pm,
    ):
        S = Sched()
        l1 = S.add("gpsimd", lambda g: g.dma_start(out=rows[0:1, 0:8], in_=po[:, :]), dma=True)
        l2 = S.add("gpsimd", lambda g: g.dma_start(out=sc[0:1, 6:7], in_=s4[:, :]), dma=True)
        l3 = S.add("gpsimd", lambda g: g.dma_start(out=sc[0:1, 5:6], in_=s2[:, :]), dma=True)

        def scales(v):
            v.tensor_reduce(sc[0:1, 0:1], rows[0:1, 0:8], AX.X, OP.max)
            v.tensor_mul(sc[0:1, 7:8], sc[0:1, 5:6], sc[0:1, 6:7])
            v.drain()
            v.tensor_mul(sc[0:1, 1:2], sc[0:1, 0:1], sc[0:1, 7:8])
            v.drain()
            v.tensor_scalar(sc[0:1, 1:2], sc[0:1, 1:2], 1.0 / 32767.0, 1e-12, OP.mult, OP.max)
            v.drain()
            v.reciprocal(sc[0:1, 2:3], sc[0:1, 1:2])
            v.drain()
            return v.tensor_mul(sc[0:1, 2:3], sc[0:1, 2:3], sc[0:1, 7:8])
        sv = S.add("vector", scales, deps=[l1, l2, l3])
        on = S.add("vector", lambda v: v.memset(ones1[:, :], 1.0))
        tm = S.add("tensor", lambda t: t.matmul(pm[:, 0:1], ones1[0:1, :], sc[0:1, 2:3],
                                                start=True, stop=True), deps=[sv, on])
        bc = S.add("scalar", lambda s: s.activation(cols[:, 0:1], pm[:, 0:1], AF.Copy), deps=[tm])
        d_al = S.add("sync", lambda sy: sy.dma_start(out=alpha[:, :], in_=sc[0:1, 1:2]),
                     deps=[sv], dma=True)
        pts, obs = [pt0, pt1], [ob0, ob1]
        vsteps, odmas = {}, {}
        for rt_i in range(RT):
            pb, ob = pts[rt_i % 2], obs[rt_i % 2]
            dd = [vsteps[rt_i - 2]] if rt_i >= 2 else []
            lp = S.add("gpsimd", lambda g, rt_i=rt_i, pb=pb: g.dma_start(
                out=pb[:, :], in_=opred[rt_i * 128:(rt_i + 1) * 128, :]), deps=dd, dma=True)
            def doq(v, pb=pb, ob=ob):
                v.tensor_scalar(it[:, :], pb[:, :], cols[:, 0:1], None, OP.mult)
                return v.tensor_copy(ob[:, :], it[:, :])
            vsteps[rt_i] = S.add("vector", doq,
                                 deps=[lp, bc] + ([odmas[rt_i - 2]] if rt_i >= 2 else []))
            odmas[rt_i] = S.add("sync", lambda sy, rt_i=rt_i, ob=ob: sy.dma_start(
                out=oint[rt_i * 128:(rt_i + 1) * 128, :], in_=ob[:, :]), deps=[vsteps[rt_i]], dma=True)
        S.add("sync", lambda sy: None, deps=[d_al, odmas[RT - 1]])
        S.emit(nc)
    return nc


def ffn0_mid3(opre_list, po, s4_list, s2_list):
    R, C = opre_list[0].shape
    nc = _cached(("f0m3", R, C), lambda: build_ffn0_mid3(R, C))
    maps = [{"opre": _f32(opre_list[c]), "po": _f32(po),
             "s4": _f32(np.asarray(s4_list[c]).reshape(1, 1)),
             "s2": _f32(np.asarray(s2_list[c]).reshape(1, 1))} for c in range(NCORE)]
    res = _run(nc, maps)
    return ([np.asarray(r["oint"]) for r in res], [np.asarray(r["alpha"]) for r in res])


def build_ffn1_actq(T, D):
    TT = T // 128
    nc = bass.Bass("TRN2", target_bir_lowering=False, debug=False)
    x = nc.dram_tensor("x", [T, D], F32, kind="ExternalInput").ap()
    xq = nc.dram_tensor("xq", [T, D], F32, kind="ExternalOutput").ap()
    rmo = nc.dram_tensor("rmout", [T, 1], F32, kind="ExternalOutput").ap()
    with (
        nc.sbuf_tensor("xt0", [128, D], F32) as xt0,
        nc.sbuf_tensor("xt1", [128, D], F32) as xt1,
        nc.sbuf_tensor("it", [128, D], I32) as it,
        nc.sbuf_tensor("ob0", [128, D], F32) as ob0,
        nc.sbuf_tensor("ob1", [128, D], F32) as ob1,
        nc.sbuf_tensor("rm", [128, 4], F32) as rm,
    ):
        S = Sched()
        xts, obs = [xt0, xt1], [ob0, ob1]
        vs, d1, d2 = {}, {}, {}
        for tt in range(TT):
            xb, ob = xts[tt % 2], obs[tt % 2]
            ld = S.add("gpsimd", lambda g, tt=tt, xb=xb: g.dma_start(
                out=xb[:, :], in_=x[tt * 128:(tt + 1) * 128, :]),
                deps=[vs[tt - 2]] if tt >= 2 else [], dma=True)
            def doq(v, xb=xb, ob=ob):
                v.tensor_reduce(rm[:, 0:1], xb[:, :], AX.X, OP.max, apply_absolute_value=True)
                v.drain()
                v.tensor_scalar(rm[:, 0:1], rm[:, 0:1], 1e-5, None, OP.max)
                v.drain()
                v.reciprocal(rm[:, 1:2], rm[:, 0:1])
                v.drain()
                v.tensor_scalar(rm[:, 1:2], rm[:, 1:2], 127.0, None, OP.mult)
                v.drain()
                v.tensor_scalar(it[:, :], xb[:, :], rm[:, 1:2], None, OP.mult)
                return v.tensor_copy(ob[:, :], it[:, :])
            vs[tt] = S.add("vector", doq,
                           deps=[ld] + ([d1[tt - 2]] if tt >= 2 else [])
                                + ([d2[tt - 1]] if tt >= 1 else []))
            d1[tt] = S.add("sync", lambda sy, tt=tt, ob=ob: sy.dma_start(
                out=xq[tt * 128:(tt + 1) * 128, :], in_=ob[:, :]), deps=[vs[tt]], dma=True)
            d2[tt] = S.add("sync", lambda sy, tt=tt: sy.dma_start(
                out=rmo[tt * 128:(tt + 1) * 128, :], in_=rm[:, 0:1]), deps=[vs[tt]], dma=True)
        S.add("sync", lambda sy: None, deps=[d1[TT - 1], d2[TT - 1]])
        S.emit(nc)
    return nc


def ffn1_actq(x_list):
    T, D = x_list[0].shape
    nc = _cached(("f1aq", T, D), lambda: build_ffn1_actq(T, D))
    maps = [{"x": _f32(x_list[c])} for c in range(NCORE)]
    res = _run(nc, maps)
    return ([np.asarray(r["xq"]) for r in res], [np.asarray(r["rmout"]) for r in res])


def build_ffn1_mid(T, D):
    TT = T // 128
    nc = bass.Bass("TRN2", target_bir_lowering=False, debug=False)
    gr = nc.dram_tensor("gr", [T, D], F32, kind="ExternalInput").ap()
    ur = nc.dram_tensor("ur", [T, D], F32, kind="ExternalInput").ap()
    rmi = nc.dram_tensor("rmi", [TT, 128], F32, kind="ExternalInput").ap()
    ident_in = nc.dram_tensor("ident", [128, 128], F32, kind="ExternalInput").ap()
    hq = nc.dram_tensor("hq", [T, D], F32, kind="ExternalOutput").ap()
    rm2o = nc.dram_tensor("rm2", [T, 1], F32, kind="ExternalOutput").ap()
    with (
        nc.sbuf_tensor("gt0", [128, D], F32) as gt0,
        nc.sbuf_tensor("gt1", [128, D], F32) as gt1,
        nc.sbuf_tensor("ut0", [128, D], F32) as ut0,
        nc.sbuf_tensor("ut1", [128, D], F32) as ut1,
        nc.sbuf_tensor("it", [128, D], I32) as it,
        nc.sbuf_tensor("sg", [128, D], F32) as sg,
        nc.sbuf_tensor("h", [128, D], F32) as h,
        nc.sbuf_tensor("ob0", [128, D], F32) as ob0,
        nc.sbuf_tensor("ob1", [128, D], F32) as ob1,
        nc.sbuf_tensor("rmsb", [TT, 128], F32) as rmsb,
        nc.sbuf_tensor("rmt", [128, TT], F32) as rmt,
        nc.sbuf_tensor("rm", [128, 4], F32) as rm,
        nc.sbuf_tensor("idn", [128, 128], F32) as idn,
        nc.psum_tensor("prm", [128, TT], F32) as prm,
    ):
        S = Sched()
        lr = S.add("gpsimd", lambda g: g.dma_start(out=rmsb[:, :], in_=rmi[:, :]), dma=True)
        li = S.add("gpsimd", lambda g: g.dma_start(out=idn[:, :], in_=ident_in[:, :]), dma=True)
        tr = S.add("tensor", lambda t: t.transpose(prm[:, 0:TT], rmsb[:, :], idn[0:TT, 0:TT]),
                   deps=[lr, li])
        cp = S.add("scalar", lambda s: s.activation(rmt[:, 0:TT], prm[:, 0:TT], AF.Copy), deps=[tr])
        gts, uts, obs = [gt0, gt1], [ut0, ut1], [ob0, ob1]
        vs, sgs, d1, d2 = {}, {}, {}, {}
        for tt in range(TT):
            gb, ub, ob = gts[tt % 2], uts[tt % 2], obs[tt % 2]
            dd = [vs[tt - 2]] if tt >= 2 else []
            l1 = S.add("gpsimd", lambda g, tt=tt, gb=gb: g.dma_start(
                out=gb[:, :], in_=gr[tt * 128:(tt + 1) * 128, :]), deps=dd, dma=True)
            l2 = S.add("gpsimd", lambda g, tt=tt, ub=ub: g.dma_start(
                out=ub[:, :], in_=ur[tt * 128:(tt + 1) * 128, :]), deps=dd, dma=True)
            def pre(v, gb=gb, tt=tt):
                v.tensor_scalar(rm[:, 2:3], rmt[:, tt:tt + 1], 1.0 / 127.0, None, OP.mult)
                v.drain()
                return v.tensor_scalar(gb[:, :], gb[:, :], rm[:, 2:3], None, OP.mult)
            pr = S.add("vector", pre, deps=[l1, cp])
            sgs[tt] = S.add("scalar", lambda s, gb=gb: s.activation(sg[:, :], gb[:, :], AF.Silu),
                            deps=[pr])
            def mid(v, ub=ub, ob=ob):
                v.tensor_scalar(ub[:, :], ub[:, :], rm[:, 2:3], None, OP.mult)
                v.drain()
                v.tensor_mul(h[:, :], sg[:, :], ub[:, :])
                v.drain()
                v.tensor_reduce(rm[:, 0:1], h[:, :], AX.X, OP.max, apply_absolute_value=True)
                v.drain()
                v.tensor_scalar(rm[:, 0:1], rm[:, 0:1], 1e-5, None, OP.max)
                v.drain()
                v.reciprocal(rm[:, 1:2], rm[:, 0:1])
                v.drain()
                v.tensor_scalar(rm[:, 1:2], rm[:, 1:2], 127.0, None, OP.mult)
                v.drain()
                v.tensor_scalar(it[:, :], h[:, :], rm[:, 1:2], None, OP.mult)
                return v.tensor_copy(ob[:, :], it[:, :])
            vs[tt] = S.add("vector", mid,
                           deps=[l2, sgs[tt]] + ([d1[tt - 2]] if tt >= 2 else [])
                                + ([d2[tt - 1]] if tt >= 1 else []))
            d1[tt] = S.add("sync", lambda sy, tt=tt, ob=ob: sy.dma_start(
                out=hq[tt * 128:(tt + 1) * 128, :], in_=ob[:, :]), deps=[vs[tt]], dma=True)
            d2[tt] = S.add("sync", lambda sy, tt=tt: sy.dma_start(
                out=rm2o[tt * 128:(tt + 1) * 128, :], in_=rm[:, 0:1]), deps=[vs[tt]], dma=True)
        S.add("sync", lambda sy: None, deps=[d1[TT - 1], d2[TT - 1]])
        S.emit(nc)
    return nc


def ffn1_mid(gr_list, ur_list, rm_list):
    T, D = gr_list[0].shape
    TT = T // 128
    nc = _cached(("f1mid", T, D), lambda: build_ffn1_mid(T, D))
    eye = np.eye(128, dtype=np.float32)
    maps = [{"gr": _f32(gr_list[c]), "ur": _f32(ur_list[c]),
             "rmi": _f32(np.asarray(rm_list[c]).reshape(TT, 128)), "ident": eye}
            for c in range(NCORE)]
    res = _run(nc, maps)
    return ([np.asarray(r["hq"]) for r in res], [np.asarray(r["rm2"]) for r in res])


def build_rowscale_add(T, D, scale_imm):
    TT = T // 128
    nc = bass.Bass("TRN2", target_bir_lowering=False, debug=False)
    x = nc.dram_tensor("x", [T, D], F32, kind="ExternalInput").ap()
    y = nc.dram_tensor("y", [T, D], F32, kind="ExternalInput").ap()
    rmi = nc.dram_tensor("rmi", [TT, 128], F32, kind="ExternalInput").ap()
    ident_in = nc.dram_tensor("ident", [128, 128], F32, kind="ExternalInput").ap()
    out = nc.dram_tensor("out", [T, D], F32, kind="ExternalOutput").ap()
    with (
        nc.sbuf_tensor("xt", [128, D], F32) as xt,
        nc.sbuf_tensor("yt", [128, D], F32) as yt,
        nc.sbuf_tensor("ob0", [128, D], F32) as ob0,
        nc.sbuf_tensor("ob1", [128, D], F32) as ob1,
        nc.sbuf_tensor("rmsb", [TT, 128], F32) as rmsb,
        nc.sbuf_tensor("rmt", [128, TT], F32) as rmt,
        nc.sbuf_tensor("mc", [128, 1], F32) as mc,
        nc.sbuf_tensor("idn", [128, 128], F32) as idn,
        nc.psum_tensor("prm", [128, TT], F32) as prm,
    ):
        S = Sched()
        lr = S.add("gpsimd", lambda g: g.dma_start(out=rmsb[:, :], in_=rmi[:, :]), dma=True)
        li = S.add("gpsimd", lambda g: g.dma_start(out=idn[:, :], in_=ident_in[:, :]), dma=True)
        tr = S.add("tensor", lambda t: t.transpose(prm[:, 0:TT], rmsb[:, :], idn[0:TT, 0:TT]),
                   deps=[lr, li])
        cp = S.add("scalar", lambda s: s.activation(rmt[:, 0:TT], prm[:, 0:TT], AF.Copy), deps=[tr])
        obs = [ob0, ob1]
        vs, dm = {}, {}
        for tt in range(TT):
            ob = obs[tt % 2]
            dd = [vs[tt - 1]] if tt >= 1 else []
            l1 = S.add("gpsimd", lambda g, tt=tt: g.dma_start(
                out=xt[:, :], in_=x[tt * 128:(tt + 1) * 128, :]), deps=dd, dma=True)
            l2 = S.add("gpsimd", lambda g, tt=tt: g.dma_start(
                out=yt[:, :], in_=y[tt * 128:(tt + 1) * 128, :]), deps=dd, dma=True)
            def apply(v, ob=ob, tt=tt):
                v.tensor_scalar(mc[:, 0:1], rmt[:, tt:tt + 1], scale_imm, None, OP.mult)
                v.drain()
                return v.scalar_tensor_tensor(ob[:, :], yt[:, :], mc[:, 0:1], xt[:, :],
                                              OP.mult, OP.add)
            vs[tt] = S.add("vector", apply,
                           deps=[l1, l2, cp] + ([dm[tt - 2]] if tt >= 2 else []))
            dm[tt] = S.add("sync", lambda sy, tt=tt, ob=ob: sy.dma_start(
                out=out[tt * 128:(tt + 1) * 128, :], in_=ob[:, :]), deps=[vs[tt]], dma=True)
        S.add("sync", lambda sy: None, deps=[dm[TT - 1]])
        S.emit(nc)
    return nc


def rowscale_add(x_list, y_list, rm_list, scale_imm):
    T, D = x_list[0].shape
    TT = T // 128
    nc = _cached(("rsadd", T, D, scale_imm), lambda: build_rowscale_add(T, D, scale_imm))
    eye = np.eye(128, dtype=np.float32)
    maps = [{"x": _f32(x_list[c]), "y": _f32(y_list[c]),
             "rmi": _f32(np.asarray(rm_list[c]).reshape(TT, 128)), "ident": eye}
            for c in range(NCORE)]
    res = _run(nc, maps)
    return [np.asarray(r["out"]) for r in res]


def build_lpbq(R, C):
    RT = R // 128
    NB = C // 32
    nc = bass.Bass("TRN2", target_bir_lowering=False, debug=False)
    w = nc.dram_tensor("w", [R, C], F32, kind="ExternalInput").ap()
    out = nc.dram_tensor("out", [R, C], F32, kind="ExternalOutput").ap()
    with (
        nc.sbuf_tensor("wt0", [128, C], F32) as wt0,
        nc.sbuf_tensor("wt1", [128, C], F32) as wt1,
        nc.sbuf_tensor("wi", [128, C], I32) as wi,
        nc.sbuf_tensor("wf", [128, C], F32) as wf,
        nc.sbuf_tensor("aw", [128, C], F32) as aw,
        nc.sbuf_tensor("ob0", [128, C], F32) as ob0,
        nc.sbuf_tensor("ob1", [128, C], F32) as ob1,
        nc.sbuf_tensor("bm", [128, 24 * NB], F32) as bm,
        nc.sbuf_tensor("sblk", [128, NB], F32) as sblk,
        nc.sbuf_tensor("sinv", [128, NB], F32) as sinv,
    ):
        S = Sched()
        wts, obs = [wt0, wt1], [ob0, ob1]
        vs, dm = {}, {}
        for rt_i in range(RT):
            wb, ob = wts[rt_i % 2], obs[rt_i % 2]
            ld = S.add("gpsimd", lambda g, rt_i=rt_i, wb=wb: g.dma_start(
                out=wb[:, :], in_=w[rt_i * 128:(rt_i + 1) * 128, :]),
                deps=[vs[rt_i - 2]] if rt_i >= 2 else [], dma=True)
            def dequant(v, wb=wb, ob=ob):
                for j in range(16):
                    v.tensor_tensor(bm[:, j * NB:(j + 1) * NB], aw[:, 2 * j::32],
                                    aw[:, 2 * j + 1::32], OP.max)
                v.drain()
                for j in range(8):
                    v.tensor_tensor(bm[:, (16 + j) * NB:(17 + j) * NB],
                                    bm[:, 2 * j * NB:(2 * j + 1) * NB],
                                    bm[:, (2 * j + 1) * NB:(2 * j + 2) * NB], OP.max)
                v.drain()
                for j in range(4):
                    v.tensor_tensor(bm[:, j * NB:(j + 1) * NB],
                                    bm[:, (16 + 2 * j) * NB:(17 + 2 * j) * NB],
                                    bm[:, (17 + 2 * j) * NB:(18 + 2 * j) * NB], OP.max)
                v.drain()
                v.tensor_tensor(bm[:, 4 * NB:5 * NB], bm[:, 0:NB], bm[:, NB:2 * NB], OP.max)
                v.tensor_tensor(bm[:, 5 * NB:6 * NB], bm[:, 2 * NB:3 * NB], bm[:, 3 * NB:4 * NB], OP.max)
                v.drain()
                v.tensor_tensor(sblk[:, :], bm[:, 4 * NB:5 * NB], bm[:, 5 * NB:6 * NB], OP.max)
                v.drain()
                v.tensor_scalar(sblk[:, :], sblk[:, :], 1.0 / 7.0, 1e-12, OP.mult, OP.max)
                v.drain()
                v.reciprocal(sinv[:, :], sblk[:, :])
                v.drain()
                for j in range(32):
                    v.tensor_tensor(wi[:, j::32], wb[:, j::32], sinv[:, :], OP.mult)
                v.drain()
                v.tensor_copy(wf[:, :], wi[:, :])
                v.drain()
                for j in range(32):
                    v.tensor_tensor(ob[:, j::32], wf[:, j::32], sblk[:, :], OP.mult)
                return v.drain()
            ab = S.add("scalar", lambda s, wb=wb: s.activation(aw[:, :], wb[:, :], AF.Abs),
                       deps=[ld] + ([vs[rt_i - 1]] if rt_i >= 1 else []))
            vs[rt_i] = S.add("vector", dequant,
                             deps=[ld, ab] + ([dm[rt_i - 2]] if rt_i >= 2 else []))
            dm[rt_i] = S.add("sync", lambda sy, rt_i=rt_i, ob=ob: sy.dma_start(
                out=out[rt_i * 128:(rt_i + 1) * 128, :], in_=ob[:, :]), deps=[vs[rt_i]], dma=True)
        S.add("sync", lambda sy: None, deps=[dm[RT - 1]])
        S.emit(nc)
    return nc


def lpbq(w_full):
    R, C = w_full.shape
    Rs = R // NCORE
    nc = _cached(("lpbq", Rs, C), lambda: build_lpbq(Rs, C))
    maps = [{"w": _f32(w_full[c * Rs:(c + 1) * Rs, :])} for c in range(NCORE)]
    res = _run(nc, maps)
    return np.concatenate([np.asarray(r["out"]) for r in res], axis=0)


def build_wqmake(R, C, numel):
    RT = R // 128
    nc = bass.Bass("TRN2", target_bir_lowering=False, debug=False)
    w = nc.dram_tensor("w", [R, C], F32, kind="ExternalInput").ap()
    parts = nc.dram_tensor("parts", [1, 8], F32, kind="ExternalInput").ap()
    out = nc.dram_tensor("out", [R, C], F32, kind="ExternalOutput").ap()
    with (
        nc.sbuf_tensor("wt0", [128, C], F32) as wt0,
        nc.sbuf_tensor("wt1", [128, C], F32) as wt1,
        nc.sbuf_tensor("wi", [128, C], I32) as wi,
        nc.sbuf_tensor("wf", [128, C], F32) as wf,
        nc.sbuf_tensor("ob0", [128, C], F32) as ob0,
        nc.sbuf_tensor("ob1", [128, C], F32) as ob1,
        nc.sbuf_tensor("pr", [1, 8], F32) as pr,
        nc.sbuf_tensor("sc", [1, 4], F32) as sc,
        nc.sbuf_tensor("cols", [128, 2], F32) as cols,
        nc.sbuf_tensor("ones1", [1, 128], F32) as ones1,
        nc.psum_tensor("pm", [128, 2], F32) as pm,
    ):
        S = Sched()
        l1 = S.add("gpsimd", lambda g: g.dma_start(out=pr[:, :], in_=parts[:, :]), dma=True)

        def scales(v):
            v.tensor_reduce(sc[0:1, 0:1], pr[0:1, :], AX.X, OP.add)
            v.drain()
            v.tensor_scalar(sc[0:1, 1:2], sc[0:1, 0:1], 1.0 / numel, None, OP.mult)
            v.drain()
            v.tensor_scalar(sc[0:1, 1:2], sc[0:1, 1:2], 1e-5, None, OP.max)
            v.drain()
            return v.reciprocal(sc[0:1, 2:3], sc[0:1, 1:2])
        sv = S.add("vector", scales, deps=[l1])
        on = S.add("vector", lambda v: v.memset(ones1[:, :], 1.0))
        tm = S.add("tensor", lambda t: t.matmul(pm[:, 0:2], ones1[0:1, :], sc[0:1, 1:3],
                                                start=True, stop=True), deps=[sv, on])
        bc = S.add("scalar", lambda s: s.activation(cols[:, 0:2], pm[:, 0:2], AF.Copy), deps=[tm])
        wts, obs = [wt0, wt1], [ob0, ob1]
        vs, dm = {}, {}
        for rt_i in range(RT):
            wb, ob = wts[rt_i % 2], obs[rt_i % 2]
            ld = S.add("gpsimd", lambda g, rt_i=rt_i, wb=wb: g.dma_start(
                out=wb[:, :], in_=w[rt_i * 128:(rt_i + 1) * 128, :]),
                deps=[vs[rt_i - 2]] if rt_i >= 2 else [], dma=True)
            def doq(v, wb=wb, ob=ob):
                v.tensor_scalar(wi[:, :], wb[:, :], cols[:, 1:2], None, OP.mult)
                v.tensor_copy(wf[:, :], wi[:, :])
                v.drain()
                v.tensor_scalar(wf[:, :], wf[:, :], 1.0, -1.0, OP.min, OP.max)
                v.drain()
                return v.tensor_scalar(ob[:, :], wf[:, :], cols[:, 0:1], None, OP.mult)
            vs[rt_i] = S.add("vector", doq,
                             deps=[ld, bc] + ([dm[rt_i - 2]] if rt_i >= 2 else []))
            dm[rt_i] = S.add("sync", lambda sy, rt_i=rt_i, ob=ob: sy.dma_start(
                out=out[rt_i * 128:(rt_i + 1) * 128, :], in_=ob[:, :]), deps=[vs[rt_i]], dma=True)
        S.add("sync", lambda sy: None, deps=[dm[RT - 1]])
        S.emit(nc)
    return nc


def wq_make(w_full):
    R, C = w_full.shape
    Rs = R // NCORE
    shards = [w_full[c * Rs:(c + 1) * Rs, :] for c in range(NCORE)]
    parts = premax([shards], op_name="add")[0]
    nc = _cached(("wqm", Rs, C, R * C), lambda: build_wqmake(Rs, C, R * C))
    maps = [{"w": _f32(shards[c]), "parts": _f32(parts)} for c in range(NCORE)]
    res = _run(nc, maps)
    return np.concatenate([np.asarray(r["out"]) for r in res], axis=0)


def build_conv(R, T):
    RT = R // 128
    nc = bass.Bass("TRN2", target_bir_lowering=False, debug=False)
    xp = nc.dram_tensor("xp", [R, T], F32, kind="ExternalInput").ap()
    cw = nc.dram_tensor("cw", [R, 4], F32, kind="ExternalInput").ap()
    cb = nc.dram_tensor("cb", [R, 1], F32, kind="ExternalInput").ap()
    out = nc.dram_tensor("out", [R, T], F32, kind="ExternalOutput").ap()
    with (
        nc.sbuf_tensor("xt0", [128, T], F32) as xt0,
        nc.sbuf_tensor("xt1", [128, T], F32) as xt1,
        nc.sbuf_tensor("acc", [128, T], F32) as acc,
        nc.sbuf_tensor("ob0", [128, T], F32) as ob0,
        nc.sbuf_tensor("ob1", [128, T], F32) as ob1,
        nc.sbuf_tensor("cwt", [128, 4], F32) as cwt,
        nc.sbuf_tensor("cbt", [128, 1], F32) as cbt,
    ):
        S = Sched()
        xts, obs = [xt0, xt1], [ob0, ob1]
        vs, ss, dm = {}, {}, {}
        for rt_i in range(RT):
            xb, ob = xts[rt_i % 2], obs[rt_i % 2]
            dd = [vs[rt_i - 2]] if rt_i >= 2 else []
            ld = S.add("gpsimd", lambda g, rt_i=rt_i, xb=xb: g.dma_start(
                out=xb[:, :], in_=xp[rt_i * 128:(rt_i + 1) * 128, :]), deps=dd, dma=True)
            lw = S.add("gpsimd", lambda g, rt_i=rt_i: g.dma_start(
                out=cwt[:, :], in_=cw[rt_i * 128:(rt_i + 1) * 128, :]),
                deps=[vs[rt_i - 1]] if rt_i >= 1 else [], dma=True)
            lb = S.add("gpsimd", lambda g, rt_i=rt_i: g.dma_start(
                out=cbt[:, :], in_=cb[rt_i * 128:(rt_i + 1) * 128, :]),
                deps=[ss[rt_i - 1]] if rt_i >= 1 else [], dma=True)
            def doconv(v, xb=xb):
                v.tensor_scalar(acc[:, :], xb[:, :], cwt[:, 3:4], None, OP.mult)
                for k in range(3):
                    sh = 3 - k
                    v.scalar_tensor_tensor(acc[:, sh:T], xb[:, 0:T - sh], cwt[:, k:k + 1],
                                           acc[:, sh:T], OP.mult, OP.add)
                return v.drain()
            vs[rt_i] = S.add("vector", doconv,
                             deps=[ld, lw] + ([ss[rt_i - 1]] if rt_i >= 1 else []))
            ss[rt_i] = S.add("scalar", lambda s, ob=ob: s.activation(
                ob[:, :], acc[:, :], AF.Silu, bias=cbt[:, 0:1]),
                deps=[vs[rt_i], lb] + ([dm[rt_i - 2]] if rt_i >= 2 else []))
            dm[rt_i] = S.add("sync", lambda sy, rt_i=rt_i, ob=ob: sy.dma_start(
                out=out[rt_i * 128:(rt_i + 1) * 128, :], in_=ob[:, :]), deps=[ss[rt_i]], dma=True)
        S.add("sync", lambda sy: None, deps=[dm[RT - 1]])
        S.emit(nc)
    return nc


def conv_silu(xp_list, cw_list, cb_list):
    R, T = xp_list[0].shape
    nc = _cached(("conv", R, T), lambda: build_conv(R, T))
    maps = [{"xp": _f32(xp_list[c]), "cw": _f32(cw_list[c]), "cb": _f32(cb_list[c].reshape(R, 1))}
            for c in range(NCORE)]
    res = _run(nc, maps)
    return [np.asarray(r["out"]) for r in res]


def build_scan(R, T):
    RT = R // 128
    nc = bass.Bass("TRN2", target_bir_lowering=False, debug=False)
    u = nc.dram_tensor("u", [R, T], F32, kind="ExternalInput").ap()
    logd = nc.dram_tensor("logd", [R, 1], F32, kind="ExternalInput").ap()
    Am = nc.dram_tensor("Am", [R, 16], F32, kind="ExternalInput").ap()
    Bm = nc.dram_tensor("Bm", [R, 16], F32, kind="ExternalInput").ap()
    out = nc.dram_tensor("out", [R, T], F32, kind="ExternalOutput").ap()
    with (
        nc.sbuf_tensor("ut0", [128, T], F32) as ut0,
        nc.sbuf_tensor("ut1", [128, T], F32) as ut1,
        nc.sbuf_tensor("abuf", [128, T], F32) as abuf,
        nc.sbuf_tensor("sb", [128, T], F32) as sb,
        nc.sbuf_tensor("ob0", [128, T], F32) as ob0,
        nc.sbuf_tensor("ob1", [128, T], F32) as ob1,
        nc.sbuf_tensor("ld_t", [128, 1], F32) as ld_t,
        nc.sbuf_tensor("amt", [128, 16], F32) as amt,
        nc.sbuf_tensor("bmt", [128, 16], F32) as bmt,
        nc.sbuf_tensor("cc", [128, 6], F32) as cc,
        nc.sbuf_tensor("ones", [128, T], F32) as ones,
    ):
        S = Sched()
        on = S.add("vector", lambda v: v.memset(ones[:, :], 1.0))
        uts, obs = [ut0, ut1], [ob0, ob1]
        vs, dm, sps = {}, {}, {}
        for rt_i in range(RT):
            ub, ob = uts[rt_i % 2], obs[rt_i % 2]
            dd = [vs[rt_i - 2]] if rt_i >= 2 else []
            lu = S.add("gpsimd", lambda g, rt_i=rt_i, ub=ub: g.dma_start(
                out=ub[:, :], in_=u[rt_i * 128:(rt_i + 1) * 128, :]), deps=dd, dma=True)
            pdep = [vs[rt_i - 1]] if rt_i >= 1 else []
            l1 = S.add("gpsimd", lambda g, rt_i=rt_i: g.dma_start(
                out=ld_t[:, :], in_=logd[rt_i * 128:(rt_i + 1) * 128, :]), deps=pdep, dma=True)
            l2 = S.add("gpsimd", lambda g, rt_i=rt_i: g.dma_start(
                out=amt[:, :], in_=Am[rt_i * 128:(rt_i + 1) * 128, :]), deps=pdep, dma=True)
            l3 = S.add("gpsimd", lambda g, rt_i=rt_i: g.dma_start(
                out=bmt[:, :], in_=Bm[rt_i * 128:(rt_i + 1) * 128, :]), deps=pdep, dma=True)
            def spc(s):
                s.activation(cc[:, 2:3], ld_t[:, 0:1], AF.Abs)
                s.drain()
                s.activation(cc[:, 3:4], cc[:, 2:3], AF.Exp, scale=-1.0)
                s.drain()
                s.activation(cc[:, 4:5], cc[:, 3:4], AF.Ln, bias=1.0)
                s.drain()
                return s.activation(cc[:, 5:6], ld_t[:, 0:1], AF.Relu)
            sp1 = S.add("scalar", spc, deps=[l1] + ([vs[rt_i - 1]] if rt_i >= 1 else []))
            def spv(v):
                v.tensor_add(cc[:, 2:3], cc[:, 4:5], cc[:, 5:6])
                return v.drain()
            sp2 = S.add("vector", spv, deps=[sp1])
            sp3 = S.add("scalar", lambda s: s.activation(cc[:, 0:1], cc[:, 2:3], AF.Exp, scale=-1.0),
                        deps=[sp2])
            def ccv(v):
                v.tensor_mul(amt[:, :], amt[:, :], bmt[:, :])
                v.drain()
                v.tensor_reduce(cc[:, 1:2], amt[:, :], AX.X, OP.add)
                return v.drain()
            cv = S.add("vector", ccv, deps=[l2, l3, sp1])
            def doscan(v, ub=ub, ob=ob):
                v.tensor_scalar(abuf[:, :], ones[:, :], cc[:, 0:1], None, OP.mult)
                v.drain()
                v.tensor_tensor_scan(sb[:, :], abuf[:, :], ub[:, :], 0.0, OP.mult, OP.add)
                v.drain()
                return v.tensor_scalar(ob[:, :], sb[:, :], cc[:, 1:2], None, OP.mult)
            vs[rt_i] = S.add("vector", doscan,
                             deps=[lu, sp3, cv, on] + ([dm[rt_i - 2]] if rt_i >= 2 else []))
            sps[rt_i] = sp1
            dm[rt_i] = S.add("sync", lambda sy, rt_i=rt_i, ob=ob: sy.dma_start(
                out=out[rt_i * 128:(rt_i + 1) * 128, :], in_=ob[:, :]), deps=[vs[rt_i]], dma=True)
        S.add("sync", lambda sy: None, deps=[dm[RT - 1]])
        S.emit(nc)
    return nc


def ssm_scan(u_list, logd_list, A_list, B_list):
    R, T = u_list[0].shape
    nc = _cached(("scan", R, T), lambda: build_scan(R, T))
    maps = [{"u": _f32(u_list[c]), "logd": _f32(logd_list[c].reshape(R, 1)),
             "Am": _f32(A_list[c]), "Bm": _f32(B_list[c])} for c in range(NCORE)]
    res = _run(nc, maps)
    return [np.asarray(r["out"]) for r in res]


def build_gatemul(T, D):
    TT = T // 128
    nc = bass.Bass("TRN2", target_bir_lowering=False, debug=False)
    a = nc.dram_tensor("a", [T, D], F32, kind="ExternalInput").ap()
    b = nc.dram_tensor("b", [T, D], F32, kind="ExternalInput").ap()
    out = nc.dram_tensor("out", [T, D], F32, kind="ExternalOutput").ap()
    with (
        nc.sbuf_tensor("at0", [128, D], F32) as at0,
        nc.sbuf_tensor("at1", [128, D], F32) as at1,
        nc.sbuf_tensor("bt0", [128, D], F32) as bt0,
        nc.sbuf_tensor("bt1", [128, D], F32) as bt1,
        nc.sbuf_tensor("sg", [128, D], F32) as sg,
        nc.sbuf_tensor("ob0", [128, D], F32) as ob0,
        nc.sbuf_tensor("ob1", [128, D], F32) as ob1,
    ):
        S = Sched()
        ats, bts, obs = [at0, at1], [bt0, bt1], [ob0, ob1]
        vs, ss, dm = {}, {}, {}
        for tt in range(TT):
            ab, bb, ob = ats[tt % 2], bts[tt % 2], obs[tt % 2]
            dd = [vs[tt - 2]] if tt >= 2 else []
            l1 = S.add("gpsimd", lambda g, tt=tt, ab=ab: g.dma_start(
                out=ab[:, :], in_=a[tt * 128:(tt + 1) * 128, :]), deps=dd, dma=True)
            l2 = S.add("gpsimd", lambda g, tt=tt, bb=bb: g.dma_start(
                out=bb[:, :], in_=b[tt * 128:(tt + 1) * 128, :]),
                deps=[ss[tt - 2]] if tt >= 2 else [], dma=True)
            ss[tt] = S.add("scalar", lambda s, bb=bb: s.activation(sg[:, :], bb[:, :], AF.Sigmoid),
                           deps=[l2] + ([vs[tt - 1]] if tt >= 1 else []))
            vs[tt] = S.add("vector", lambda v, ab=ab, ob=ob: v.tensor_mul(ob[:, :], ab[:, :], sg[:, :]),
                           deps=[l1, ss[tt]] + ([dm[tt - 2]] if tt >= 2 else []))
            dm[tt] = S.add("sync", lambda sy, tt=tt, ob=ob: sy.dma_start(
                out=out[tt * 128:(tt + 1) * 128, :], in_=ob[:, :]), deps=[vs[tt]], dma=True)
        S.add("sync", lambda sy: None, deps=[dm[TT - 1]])
        S.emit(nc)
    return nc


def gatemul(a_list, b_list):
    T, D = a_list[0].shape
    nc = _cached(("gmul", T, D), lambda: build_gatemul(T, D))
    maps = [{"a": _f32(a_list[c]), "b": _f32(b_list[c])} for c in range(NCORE)]
    res = _run(nc, maps)
    return [np.asarray(r["out"]) for r in res]


def build_attn(kind, U, dk, dh, SL, esc):
    NT = SL // 128
    nc = bass.Bass("TRN2", target_bir_lowering=False, debug=False)
    q_in = nc.dram_tensor("q", [dk * 128, U * SL], F32, kind="ExternalInput").ap()
    k_in = nc.dram_tensor("k", [dk * 128, U * SL], F32, kind="ExternalInput").ap()
    v_in = nc.dram_tensor("v", [SL, U * dh], F32, kind="ExternalInput").ap()
    ident_in = nc.dram_tensor("ident", [128, 128], F32, kind="ExternalInput").ap()
    out = nc.dram_tensor("out", [SL, U * dh], F32, kind="ExternalOutput").ap()
    if kind == "l0a":
        wsv_in = nc.dram_tensor("wsv", [1, U], F32, kind="ExternalInput").ap()
        dxp_in = nc.dram_tensor("dxp", [NT, 128], F32, kind="ExternalInput").ap()
        l2b_in = nc.dram_tensor("l2b", [NT, 1], F32, kind="ExternalInput").ap()
        dbase_in = nc.dram_tensor("dbase", [128, 512], F32, kind="ExternalInput").ap()
        ddiag_in = nc.dram_tensor("ddiag", [128, 512], F32, kind="ExternalInput").ap()
    elif kind == "causal":
        caus_in = nc.dram_tensor("caus", [128, 128], F32, kind="ExternalInput").ap()
    elif kind == "band":
        bmask_in = nc.dram_tensor("bmask", [128, 15 * 128], F32, kind="ExternalInput").ap()

    MAXCH = {"l0a": 8, "causal": 9, "band": 3}[kind]
    import contextlib
    with contextlib.ExitStack() as _st:
        qsb = _st.enter_context(nc.sbuf_tensor("qsb", [128, dk * U * SL], BF16))
        ksb = _st.enter_context(nc.sbuf_tensor("ksb", [128, dk * U * SL], BF16))
        vsb = _st.enter_context(nc.sbuf_tensor("vsb", [128, NT * U * dh], BF16))
        idn = _st.enter_context(nc.sbuf_tensor("idn", [128, 128], F32))
        idb = _st.enter_context(nc.sbuf_tensor("idb", [128, 128], BF16))
        Psb = _st.enter_context(nc.sbuf_tensor("Psb", [128, 512], BF16))
        dbs = _st.enter_context(nc.sbuf_tensor("dbs", [128, 512] if kind == "l0a" else [1, 1], F32))
        dds = _st.enter_context(nc.sbuf_tensor("dds", [128, 512] if kind == "l0a" else [1, 1], F32))
        Ptsb = _st.enter_context(nc.sbuf_tensor("Ptsb", [128, 512], BF16))
        ssb = _st.enter_context(nc.sbuf_tensor("ssb", [128, 512], F32))
        mskb = _st.enter_context(nc.sbuf_tensor("msk", [128, 512], F32))
        scol = _st.enter_context(nc.sbuf_tensor("scol", [128, MAXCH], F32))
        sumr = _st.enter_context(nc.sbuf_tensor("sumr", [128, 2], F32))
        ob0 = _st.enter_context(nc.sbuf_tensor("ob0", [128, dh], F32))
        ob1 = _st.enter_context(nc.sbuf_tensor("ob1", [128, dh], F32))
        dxsb = _st.enter_context(nc.sbuf_tensor("dxsb", [NT, 128] if kind == "l0a" else [1, 1], F32))
        l2bc = _st.enter_context(nc.sbuf_tensor("l2bc", [NT, 1] if kind == "l0a" else [1, 1], F32))
        sgm = _st.enter_context(nc.sbuf_tensor("sgm", [NT, 128] if kind == "l0a" else [1, 1], F32))
        thrb = _st.enter_context(nc.sbuf_tensor("thrb", [128, NT] if kind == "l0a" else [1, 1], F32))
        thrc = _st.enter_context(nc.sbuf_tensor("thrc", [128, 2] if kind == "l0a" else [1, 1], F32))
        wsr = _st.enter_context(nc.sbuf_tensor("wsr", [1, U] if kind == "l0a" else [1, 1], F32))
        wsc = _st.enter_context(nc.sbuf_tensor("wsc", [128, U] if kind == "l0a" else [1, 1], F32))
        onsr = _st.enter_context(nc.sbuf_tensor("onsr", [1, 128] if kind == "l0a" else [1, 1], F32))
        caus = _st.enter_context(nc.sbuf_tensor("causb", [128, 128] if kind == "causal" else [1, 1], F32))
        bmk = _st.enter_context(nc.sbuf_tensor("bmk", [128, 15 * 128] if kind == "band" else [1, 1], F32))
        ps_s0 = _st.enter_context(nc.psum_tensor("ps_s0", [128, 512], F32))
        ps_s1 = _st.enter_context(nc.psum_tensor("ps_s1", [128, 512], F32))
        ps_o = _st.enter_context(nc.psum_tensor("ps_o", [128, dh], F32))
        ps_t = _st.enter_context(nc.psum_tensor("ps_t", [128, 512], BF16))
        ps_x = _st.enter_context(nc.psum_tensor("ps_x", [128, NT] if kind == "l0a" else [1, 1], F32))
        S = Sched()
        loads = []
        for kt in range(dk):
            loads.append(S.add("gpsimd", lambda g, kt=kt: g.dma_start(
                out=qsb[:, kt * U * SL:(kt + 1) * U * SL],
                in_=q_in[kt * 128:(kt + 1) * 128, :]), dma=True))
            loads.append(S.add("gpsimd", lambda g, kt=kt: g.dma_start(
                out=ksb[:, kt * U * SL:(kt + 1) * U * SL],
                in_=k_in[kt * 128:(kt + 1) * 128, :]), dma=True))
        for vt in range(NT):
            loads.append(S.add("gpsimd", lambda g, vt=vt: g.dma_start(
                out=vsb[:, vt * U * dh:(vt + 1) * U * dh],
                in_=v_in[vt * 128:(vt + 1) * 128, :]), dma=True))
        li = S.add("gpsimd", lambda g: g.dma_start(out=idn[:, :], in_=ident_in[:, :]), dma=True)
        lib = S.add("gpsimd", lambda g: g.dma_start(out=idb[:, :], in_=ident_in[:, :]), dma=True)
        loads.append(li)
        loads.append(lib)
        pre = []
        if kind == "l0a":
            l1 = S.add("gpsimd", lambda g: g.dma_start(out=dxsb[:, :], in_=dxp_in[:, :]), dma=True)
            l2 = S.add("gpsimd", lambda g: g.dma_start(out=l2bc[:, :], in_=l2b_in[:, :]), dma=True)
            l3 = S.add("gpsimd", lambda g: g.dma_start(out=dbs[:, :], in_=dbase_in[:, :]), dma=True)
            l4 = S.add("gpsimd", lambda g: g.dma_start(out=dds[:, :], in_=ddiag_in[:, :]), dma=True)
            sg = S.add("scalar", lambda s: s.activation(sgm[:, :], dxsb[:, :], AF.Sigmoid,
                                                        bias=l2bc[:, 0:1]), deps=[l1, l2])
            sc4 = S.add("vector", lambda v: v.tensor_scalar(sgm[:, :], sgm[:, :], 4096.0, None,
                                                            OP.mult), deps=[sg])
            tx = S.add("tensor", lambda t: t.transpose(ps_x[:, 0:NT], sgm[:, :], idn[0:NT, 0:NT]),
                       deps=[sc4, li])
            th = S.add("scalar", lambda s: s.activation(thrb[:, 0:NT], ps_x[:, 0:NT], AF.Copy),
                       deps=[tx])
            lw = S.add("gpsimd", lambda g: g.dma_start(out=wsr[:, :], in_=wsv_in[:, :]), dma=True)
            onr = S.add("vector", lambda v: v.memset(onsr[:, :], 1.0))
            twv = S.add("tensor", lambda t: t.matmul(ps_x[:, 0:U], onsr[0:1, :], wsr[0:1, 0:U],
                                                     start=True, stop=True), deps=[lw, onr, th])
            thw = S.add("scalar", lambda s: s.activation(wsc[:, 0:U], ps_x[:, 0:U], AF.Copy),
                        deps=[twv])
            pre = [th, thw, l3, l4]
            dyn_base = {"dyn": dbs, "dyn_diag": dds}
        elif kind == "causal":
            lc = S.add("gpsimd", lambda g: g.dma_start(out=caus[:, :], in_=caus_in[:, :]), dma=True)
            pre = [lc]
        elif kind == "band":
            lb = S.add("gpsimd", lambda g: g.dma_start(out=bmk[:, :], in_=bmask_in[:, :]), dma=True)
            pre = [lb]

        psums = [ps_s0, ps_s1]
        obs = [ob0, ob1]
        prev_scale = {}
        prev_dma = {}
        ut_i = 0
        for u in range(U):
            for t in range(NT):
                i0 = t * 128
                chunks = []
                if kind == "l0a":
                    j0 = i0
                    first = True
                    while j0 < SL:
                        w = min(512, SL - j0)
                        chunks.append((j0, w, "dyn_diag" if first else "dyn"))
                        first = False
                        j0 += w
                elif kind == "causal":
                    j0 = 0
                    while j0 < i0:
                        w = min(512, i0 - j0)
                        chunks.append((j0, w, "none"))
                        j0 += w
                    chunks.append((i0, 128, "caus"))
                elif kind == "band":
                    v5 = t % 5
                    if t > 0:
                        chunks.append((i0 - 128, 128, ("bm", v5 * 3 + 0)))
                    chunks.append((i0, 128, ("bm", v5 * 3 + 1)))
                    if t < NT - 1:
                        chunks.append((i0 + 128, 128, ("bm", v5 * 3 + 2)))
                nch = len(chunks)
                pv_steps = []
                exp_steps = []
                prev_tp = prev_cp = prev_pv = None
                for ci, (j0, w, mode) in enumerate(chunks):
                    pb = psums[ci % 2]
                    def do_sc(tt, u=u, t=t, j0=j0, w=w, pb=pb):
                        last = None
                        for kt in range(dk):
                            last = tt.matmul(pb[:, 0:w],
                                             qsb[:, kt * U * SL + u * SL + t * 128:
                                                 kt * U * SL + u * SL + t * 128 + 128],
                                             ksb[:, kt * U * SL + u * SL + j0:
                                                 kt * U * SL + u * SL + j0 + w],
                                             start=(kt == 0), stop=(kt == dk - 1))
                        return last
                    scdeps = list(loads) + pre
                    if ci >= 2:
                        scdeps.append(exp_steps[ci - 2])
                    if ci < 2 and ut_i >= 1:
                        scdeps.append(prev_scale[ut_i - 1])
                    sc_s = S.add("tensor", do_sc, deps=scdeps)
                    edeps = [sc_s]
                    if prev_tp is not None:
                        edeps.append(prev_tp)
                    if ci == 0 and ut_i >= 1:
                        edeps.append(prev_scale[ut_i - 1])
                    if mode == "none":
                        def do_exp(s, pb=pb, w=w, ci=ci):
                            return s.activation(Psb[:, 0:w], pb[:, 0:w], AF.Exp, scale=esc,
                                                accum_out=scol[:, ci:ci + 1])
                        e_s = S.add("scalar", do_exp, deps=edeps)
                    else:
                        if mode in ("dyn", "dyn_diag"):
                            base = dyn_base[mode]
                            c0 = float(i0 - j0)
                            def do_msk(v, base=base, c0=c0, t=t, w=w, pb=pb, u=u):
                                v.tensor_scalar(thrc[:, 0:1], thrb[:, t:t + 1],
                                                wsc[:, u:u + 1], c0, OP.add, OP.add)
                                v.drain()
                                v.tensor_scalar(mskb[:, 0:w], base[:, 0:w], thrc[:, 0:1], NEG,
                                                OP.is_gt, OP.mult)
                                v.drain()
                                return v.scalar_tensor_tensor(ssb[:, 0:w], pb[:, 0:w], 1.0,
                                                              mskb[:, 0:w], OP.mult, OP.add)
                            m_s = S.add("vector", do_msk, deps=edeps)
                            srcbuf = ssb
                        elif mode == "caus":
                            def do_msk(v, pb=pb, w=w):
                                return v.scalar_tensor_tensor(ssb[:, 0:w], pb[:, 0:w], 1.0,
                                                              caus[:, 0:w], OP.mult, OP.add)
                            m_s = S.add("vector", do_msk, deps=edeps)
                            srcbuf = ssb
                        else:
                            vi = mode[1]
                            def do_msk(v, pb=pb, w=w, vi=vi):
                                return v.scalar_tensor_tensor(ssb[:, 0:w], pb[:, 0:w], 1.0,
                                                              bmk[:, vi * 128:vi * 128 + w],
                                                              OP.mult, OP.add)
                            m_s = S.add("vector", do_msk, deps=edeps)
                            srcbuf = ssb
                        def do_exp(s, w=w, ci=ci, srcbuf=srcbuf):
                            return s.activation(Psb[:, 0:w], srcbuf[:, 0:w], AF.Exp, scale=esc,
                                                accum_out=scol[:, ci:ci + 1])
                        e_s = S.add("scalar", do_exp, deps=[m_s])
                    exp_steps.append(e_s)
                    nsub = w // 128
                    def do_tp(tt, nsub=nsub):
                        last = None
                        for si in range(nsub):
                            last = tt.transpose(ps_t[:, si * 128:(si + 1) * 128],
                                                Psb[:, si * 128:(si + 1) * 128], idb[:, :])
                        return last
                    tp_s = S.add("tensor", do_tp,
                                 deps=[e_s] + ([prev_cp] if prev_cp is not None else []))
                    cp_s = S.add("scalar", lambda s, nsub=nsub: s.activation(
                        Ptsb[:, 0:nsub * 128], ps_t[:, 0:nsub * 128], AF.Copy),
                        deps=[tp_s] + ([prev_pv] if prev_pv is not None else []))
                    def do_pv(tt, j0=j0, nsub=nsub, u=u, ci=ci, nch=nch):
                        last = None
                        for si in range(nsub):
                            kt_glob = (j0 + si * 128) // 128
                            last = tt.matmul(ps_o[:, 0:dh],
                                             Ptsb[:, si * 128:(si + 1) * 128],
                                             vsb[:, kt_glob * U * dh + u * dh:
                                                 kt_glob * U * dh + u * dh + dh],
                                             start=(ci == 0 and si == 0),
                                             stop=(ci == nch - 1 and si == nsub - 1))
                        return last
                    pvdeps = [cp_s]
                    if ci == 0 and ut_i >= 1:
                        pvdeps.append(prev_scale[ut_i - 1])
                    pv_s = S.add("tensor", do_pv, deps=pvdeps)
                    pv_steps.append(pv_s)
                    prev_tp, prev_cp, prev_pv = tp_s, cp_s, pv_s
                ob = obs[ut_i % 2]
                sdeps = [pv_steps[-1], exp_steps[-1]]
                if ut_i >= 2:
                    sdeps.append(prev_dma[ut_i - 2])
                def do_scale(v, nch=nch, ob=ob):
                    v.tensor_reduce(sumr[:, 0:1], scol[:, 0:nch], AX.X, OP.add)
                    v.drain()
                    v.reciprocal(sumr[:, 1:2], sumr[:, 0:1])
                    v.drain()
                    return v.tensor_scalar(ob[:, 0:dh], ps_o[:, 0:dh], sumr[:, 1:2], None, OP.mult)
                sc_step = S.add("vector", do_scale, deps=sdeps)
                prev_scale[ut_i] = sc_step
                prev_dma[ut_i] = S.add("sync", lambda sy, t=t, u=u, ob=ob: sy.dma_start(
                    out=out[t * 128:(t + 1) * 128, u * dh:(u + 1) * dh], in_=ob[:, 0:dh]),
                    deps=[sc_step], dma=True)
                ut_i += 1
        S.add("sync", lambda sy: None, deps=[prev_dma[ut_i - 1]])
        S.emit(nc)
    return nc


def attn(kind, q_list, k_list, v_list, dh, esc, ws_lists=None, dxp_list=None, l2b=None,
         caus_np=None, bmask_np=None, dbase_np=None, ddiag_np=None):
    dk128, USL = q_list[0].shape
    SL, Udh = v_list[0].shape
    U = Udh // dh
    dk = dk128 // 128
    nc = _cached(("attn", kind, U, dk, dh, SL, esc),
                 lambda: build_attn(kind, U, dk, dh, SL, esc))
    eye = np.eye(128, dtype=np.float32)
    maps = []
    for c in range(NCORE):
        m = {"q": _f32(q_list[c]), "k": _f32(k_list[c]), "v": _f32(v_list[c]), "ident": eye}
        if kind == "l0a":
            m["wsv"] = _f32(np.asarray(ws_lists[c]).reshape(1, U))
            m["dxp"] = _f32(dxp_list[c])
            m["l2b"] = _f32(np.broadcast_to(np.asarray(l2b).reshape(1, 1), (SL // 128, 1)))
            m["dbase"] = _f32(dbase_np)
            m["ddiag"] = _f32(ddiag_np)
        elif kind == "causal":
            m["caus"] = _f32(caus_np)
        elif kind == "band":
            m["bmask"] = _f32(bmask_np)
        maps.append(m)
    res = _run(nc, maps)
    return [np.asarray(r["out"]) for r in res]


B, SEQ, DM = 4, 4096, 1024
TOK = B * SEQ
TPC = TOK // NCORE


def _shard_tok(x):
    return [x[c * TPC:(c + 1) * TPC] for c in range(NCORE)]


def _shard_tokT(x):
    return [np.ascontiguousarray(x[c * TPC:(c + 1) * TPC].T) for c in range(NCORE)]


def _gather_fm(outs, M):
    return np.concatenate([o.T for o in outs], axis=0)


def _l0a_masks():
    p = np.arange(128)[:, None]
    c = np.arange(512)[None, :]
    dbase = (c - p).astype(np.float32)
    dd = dbase.copy()
    dd[c < p] = 4.0e9
    return dbase, dd


def _caus_mask():
    p = np.arange(128)[:, None]
    c = np.arange(128)[None, :]
    return np.where(c <= p, 0.0, NEG).astype(np.float32)


def _band_masks():
    CH, LC = 10, 25
    out = np.zeros((128, 15 * 128), np.float32)
    for v in range(5):
        t = v
        i = (128 * t + np.arange(128))[:, None]
        for rel in range(3):
            j0 = 128 * (t + rel - 1)
            j = j0 + np.arange(128)[None, :]
            cs = (i // CH) * CH
            ce = cs + CH
            allowed = ((j >= cs) & (j < ce)) | ((j >= i - LC) & (j < cs))
            m = np.where(allowed, 0.0, NEG).astype(np.float32)
            out[:, (v * 3 + rel) * 128:(v * 3 + rel + 1) * 128] = m
    return out


def _attn_pack_qk(full_fm, heads_per_core, dh, dk):
    outs = []
    for c in range(NCORE):
        b = c // 2
        hsel = heads_per_core[c]
        U = len(hsel)
        arr = np.zeros((dk * 128, U * SEQ), np.float32)
        for ui, h in enumerate(hsel):
            blk = full_fm[h * dh:(h + 1) * dh, b * SEQ:(b + 1) * SEQ]
            for kt in range(dk):
                rows = blk[kt * 128:min((kt + 1) * 128, dh)]
                arr[kt * 128:kt * 128 + rows.shape[0], ui * SEQ:(ui + 1) * SEQ] = rows
        outs.append(arr)
    return outs


def _attn_pack_v(full_fm, heads_per_core, dh):
    outs = []
    for c in range(NCORE):
        b = c // 2
        hsel = heads_per_core[c]
        arr = np.concatenate([full_fm[h * dh:(h + 1) * dh, b * SEQ:(b + 1) * SEQ].T
                              for h in hsel], axis=1)
        outs.append(np.ascontiguousarray(arr))
    return outs


def _attn_unpack(outs, heads_per_core, dh, NH):
    full = np.zeros((TOK, NH * dh), np.float32)
    for c in range(NCORE):
        b = c // 2
        for ui, h in enumerate(heads_per_core[c]):
            full[b * SEQ:(b + 1) * SEQ, h * dh:(h + 1) * dh] = outs[c][:, ui * dh:(ui + 1) * dh]
    return full


def _l0ffn(x_tok, w_gate_dq, w_up_dq, w_down_dq):
    xs = _shard_tok(x_tok)
    p1 = premax([xs])[0]
    ones = [np.float32(1.0)] * NCORE
    xq, s1 = quant_global(xs, p1, ones, 32767.0)
    xq_full = np.concatenate(xq, axis=0)
    rhs = _shard_tokT(xq_full)
    upr = mm(np.ascontiguousarray(w_up_dq.T), rhs)
    gtr = mm(np.ascontiguousarray(w_gate_dq.T), rhs)
    pu, pg = premax([upr, gtr])
    upq, pint, pp, s2, s3 = ffn0_mid1(upr, gtr, pu, pg, s1)
    opre, po, s4 = ffn0_mid2(pint, upq, pp, s2, s3)
    oint, alpha = ffn0_mid3(opre, po, s4, s2)
    oint_tok = _gather_fm(oint, 3072)
    dr = mm(np.ascontiguousarray(w_down_dq.T), _shard_tokT(oint_tok))
    out_tok = _gather_fm(dr, 1024)
    return out_tok, alpha


def _l1ffn(x_tok, wq_g, wq_u, wq_d):
    xs = _shard_tok(x_tok)
    xq, rm = ffn1_actq(xs)
    xq_full = np.concatenate(xq, axis=0)
    lhsT = _shard_tokT(xq_full)
    gtr = mm(lhsT, [np.ascontiguousarray(wq_g.T)] * NCORE)
    upr = mm(lhsT, [np.ascontiguousarray(wq_u.T)] * NCORE)
    hq, rm2 = ffn1_mid(gtr, upr, rm)
    hq_full = np.concatenate(hq, axis=0)
    dr = mm(_shard_tokT(hq_full), [np.ascontiguousarray(wq_d.T)] * NCORE)
    return dr, rm2


_TRACE = {}
_TRACING = False


def kernel(**inp):
    x = _f32(np.asarray(inp["L0_attn_a_query"]).reshape(TOK, DM))
    def tr(name, val):
        if _TRACING:
            _TRACE[name] = np.array(val)

    w_gate_dq = lpbq(_f32(inp["L0_ffn_w_gate"]))
    w_up_dq = lpbq(_f32(inp["L0_ffn_w_up"]))
    w_down_dq = lpbq(_f32(inp["L0_ffn_w_down"]))
    wq_g = wq_make(_f32(inp["L1_ffn_w_gate"]))
    wq_u = wq_make(_f32(inp["L1_ffn_w_up"]))
    wq_d = wq_make(_f32(inp["L1_ffn_w_down"]))
    weff = mm(_f32(inp["L0_attn_a_off_w"]),
              [_f32(inp["L0_attn_a_lin2_w"]).reshape(256, 1)] * NCORE)[0]

    dbase, ddiag = _l0a_masks()
    caus = _caus_mask()
    bmask = _band_masks()

    xln = np.concatenate(ln(_shard_tok(x)), axis=0)
    qkv = _gather_fm(mm(np.ascontiguousarray(_f32(inp["L0_attn_a_qkv_w"]).T),
                        _shard_tokT(xln), bias_np=_f32(inp["L0_attn_a_qkv_b"])), 3072).T
    dxp = _gather_fm(mm(weff, _shard_tokT(xln), rhs_relu=True), 1).reshape(TOK)
    heads = [[2 * (c % 2), 2 * (c % 2) + 1] for c in range(NCORE)]
    qf = _attn_pack_qk(qkv[0:1024], heads, 256, 2)
    kf = _attn_pack_qk(qkv[1024:2048], heads, 256, 2)
    vf = _attn_pack_v(qkv[2048:3072], heads, 256)
    dxp_list = [np.ascontiguousarray(dxp[(c // 2) * SEQ:(c // 2 + 1) * SEQ].reshape(32, 128))
                for c in range(NCORE)]
    ws_pc = [4.0, 16.0, 64.0, 256.0]
    ao = attn("l0a", qf, kf, vf, 256, 2.0 / 64.0,
              ws_lists=[[ws_pc[h] for h in heads[c]] for c in range(NCORE)],
              dxp_list=dxp_list, l2b=_f32(inp["L0_attn_a_lin2_b"]).reshape(1, 1),
              dbase_np=dbase, ddiag_np=ddiag)
    att_full = _attn_unpack(ao, heads, 256, 4)
    x = np.concatenate(resadd(_shard_tok(x), _shard_tok(att_full)), axis=0)

    tr("s1", x)

    xln = np.concatenate(ln(_shard_tok(x)), axis=0)
    f0, alpha = _l0ffn(xln, w_gate_dq, w_up_dq, w_down_dq)
    x = np.concatenate(resadd(_shard_tok(x), _shard_tok(f0), alpha_list=alpha), axis=0)

    tr("s2", x)

    xln = np.concatenate(ln(_shard_tok(x)), axis=0)
    wqkv = np.concatenate([_f32(inp["L0_attn_b_Wqk"]), _f32(inp["L0_attn_b_Wv"])], axis=1)
    bqkv = np.concatenate([_f32(inp["L0_attn_b_bqk"]), _f32(inp["L0_attn_b_bv"])])
    qkvb = _gather_fm(mm(wqkv, _shard_tokT(xln), bias_np=bqkv), 3072).T
    heads8 = [list(range(8 * (c % 2), 8 * (c % 2) + 8)) for c in range(NCORE)]
    qf = _attn_pack_qk(qkvb[0:1024], heads8, 64, 1)
    kf = _attn_pack_qk(qkvb[1024:2048], heads8, 64, 1)
    vf = _attn_pack_v(qkvb[2048:3072], heads8, 64)
    ao = attn("causal", qf, kf, vf, 64, 1.0 / 8.0, caus_np=caus)
    att_full = _attn_unpack(ao, heads8, 64, 16)
    x = np.concatenate(resadd(_shard_tok(x), _shard_tok(att_full)), axis=0)

    tr("s3", x)

    xln = np.concatenate(ln(_shard_tok(x)), axis=0)
    f0, alpha = _l0ffn(xln, w_gate_dq, w_up_dq, w_down_dq)
    x = np.concatenate(resadd(_shard_tok(x), _shard_tok(f0), alpha_list=alpha), axis=0)

    tr("s4", x)

    xln = np.concatenate(ln(_shard_tok(x)), axis=0)
    xln2 = np.concatenate(ln(_shard_tok(xln), g=_f32(inp["L1_attn_a_ln_g"]),
                             b=_f32(inp["L1_attn_a_ln_b"])), axis=0)
    xpg = _gather_fm(mm(np.ascontiguousarray(_f32(inp["L1_attn_a_W_in"]).T),
                        _shard_tokT(xln2), bias_np=_f32(inp["L1_attn_a_b_in"])), 4096).T
    E = 2048
    def ch_shard(fm):
        outs = []
        for c in range(NCORE):
            sl = fm[c * 256:(c + 1) * 256].reshape(256, B, SEQ)
            outs.append(np.ascontiguousarray(sl.transpose(1, 0, 2).reshape(B * 256, SEQ)))
        return outs
    def ch_unshard(lst):
        full = np.zeros((E, TOK), np.float32)
        for c in range(NCORE):
            sl = np.asarray(lst[c]).reshape(B, 256, SEQ).transpose(1, 0, 2).reshape(256, TOK)
            full[c * 256:(c + 1) * 256] = sl
        return full
    def rep4(vec):
        return [np.ascontiguousarray(np.tile(np.asarray(vec).reshape(-1)[c * 256:(c + 1) * 256],
                                             (B, 1)).reshape(-1)) for c in range(NCORE)]
    def rep4m(mat):
        return [np.ascontiguousarray(np.tile(np.asarray(mat)[c * 256:(c + 1) * 256], (B, 1)))
                for c in range(NCORE)]
    xp_sh = ch_shard(xpg[0:E])
    cw = rep4m(_f32(inp["L1_attn_a_conv_w"]).reshape(E, 4))
    cb = rep4(_f32(inp["L1_attn_a_conv_b"]))
    xc_sh = conv_silu(xp_sh, cw, [b_.reshape(-1, 1) for b_ in cb])
    xc_full = ch_unshard(xc_sh)
    uc = mm(np.ascontiguousarray(_f32(inp["L1_attn_a_W_c"]).T),
            [np.ascontiguousarray(xc_full[:, c * TPC:(c + 1) * TPC]) for c in range(NCORE)],
            bias_np=_f32(inp["L1_attn_a_b_c"]))
    ud = mm(np.ascontiguousarray(_f32(inp["L1_attn_a_W_d"]).T),
            [np.ascontiguousarray(xc_full[:, c * TPC:(c + 1) * TPC]) for c in range(NCORE)],
            bias_np=_f32(inp["L1_attn_a_b_d"]))
    u_full = np.concatenate([u_.T for u_ in uc], axis=0)
    ud_full = np.concatenate([u_.T for u_ in ud], axis=0)
    u_tok = gatemul(_shard_tok(u_full), _shard_tok(ud_full))
    u_fm = np.concatenate(u_tok, axis=0).T
    u_sh = ch_shard(np.ascontiguousarray(u_fm))
    ld = rep4(_f32(inp["L1_attn_a_log_delta"]))
    Am = rep4m(_f32(inp["L1_attn_a_A"]))
    Bmm = rep4m(_f32(inp["L1_attn_a_Bm"]))
    y_sh = ssm_scan(u_sh, [l.reshape(-1, 1) for l in ld], Am, Bmm)
    y_full = ch_unshard(y_sh).T
    gate_tok = np.ascontiguousarray(xpg[E:2 * E].T)
    xg = gatemul(_shard_tok(np.ascontiguousarray(y_full)), _shard_tok(gate_tok))
    xg_full = np.concatenate(xg, axis=0)
    wo = mm(np.ascontiguousarray(_f32(inp["L1_attn_a_W_out"]).T), _shard_tokT(xg_full),
            bias_np=_f32(inp["L1_attn_a_b_out"]))
    wo_tok = _gather_fm(wo, 1024)
    x = np.concatenate(resadd(_shard_tok(x), _shard_tok(wo_tok), c_list=_shard_tok(xln)), axis=0)

    tr("s5", x)

    xln = np.concatenate(ln(_shard_tok(x)), axis=0)
    dr, rm2 = _l1ffn(xln, wq_g, wq_u, wq_d)
    x = np.concatenate(rowscale_add(_shard_tok(x), dr, rm2, 1.0 / 127.0), axis=0)

    tr("s6", x)

    xln = np.concatenate(ln(_shard_tok(x)), axis=0)
    wqkv1 = np.concatenate([_f32(inp["L1_attn_b_Wq"]).T, _f32(inp["L1_attn_b_Wk"]).T,
                            _f32(inp["L1_attn_b_Wv"]).T], axis=1)
    bqkv1 = np.concatenate([_f32(inp["L1_attn_b_bq"]), _f32(inp["L1_attn_b_bk"]),
                            _f32(inp["L1_attn_b_bv"])])
    qkvb = _gather_fm(mm(np.ascontiguousarray(wqkv1), _shard_tokT(xln), bias_np=bqkv1), 3072).T
    qf = _attn_pack_qk(qkvb[0:1024], heads8, 64, 1)
    kf = _attn_pack_qk(qkvb[1024:2048], heads8, 64, 1)
    vf = _attn_pack_v(qkvb[2048:3072], heads8, 64)
    ao = attn("band", qf, kf, vf, 64, 1.0 / 8.0, bmask_np=bmask)
    att_full = _attn_unpack(ao, heads8, 64, 16)
    wob = mm(np.ascontiguousarray(_f32(inp["L1_attn_b_Wo"]).T),
             _shard_tokT(att_full), bias_np=_f32(inp["L1_attn_b_bo"]))
    wob_tok = _gather_fm(wob, 1024)
    x = np.concatenate(resadd(_shard_tok(x), _shard_tok(wob_tok)), axis=0)

    tr("s7", x)

    xln = np.concatenate(ln(_shard_tok(x)), axis=0)
    dr, rm2 = _l1ffn(xln, wq_g, wq_u, wq_d)
    x = np.concatenate(rowscale_add(_shard_tok(x), dr, rm2, 1.0 / 127.0), axis=0)

    tr("s8", x)

    return x.reshape(B, SEQ, DM)
